# version 03 (final)
# baseline (speedup 1.0000x reference)
import numpy as np
from contextlib import ExitStack

import concourse.bass as bass
import concourse.mybir as mybir
from concourse.bass_utils import run_bass_kernel_spmd

F32 = mybir.dt.float32
BF16 = mybir.dt.bfloat16
I32 = mybir.dt.int32
AF = mybir.ActivationFunctionType
OP = mybir.AluOpType
IOOA = bass.IndirectOffsetOnAxis

P = 128
DM = 1024
DI = 2048
NST = 16
DTR = 64
LSEQ = 2048
NSEQ = 2
TPC = LSEQ * NSEQ
NCORES = 8
NHEAD = 16
HD = 64
NEXP = 8
CAPG = 8602
CAPL = 1792
TRASH = NEXP * CAPL
EPS32 = float(np.finfo(np.float32).eps)


def _expand_kron(G1, G2):
    As, Bs = [], []
    for c in range(4):
        a = G1[0, :, c, :]
        b = G2[c, :, 0, :]
        Aexp = np.repeat(np.repeat(a.T, 16, axis=0), 16, axis=1)
        Bexp = np.tile(np.tile(b.T, (32, 1)), (1, 32))
        As.append(np.ascontiguousarray(Aexp, dtype=np.float32))
        Bs.append(np.ascontiguousarray(Bexp, dtype=np.float32))
    return As, Bs


def _host_prepare(inp):
    shared = {}
    f32 = np.float32

    shared["c_ident_b"] = np.eye(P, dtype=f32)
    shared["c_ident_f"] = np.eye(P, dtype=f32)
    shared["c_ones_row"] = np.ones((1, P), dtype=f32)
    shared["c_ones_col"] = np.ones((P, 1), dtype=f32)
    shared["c_lt_strict"] = np.triu(np.ones((P, P), dtype=f32), 1)
    st_sx = np.zeros((P, 2), dtype=f32); st_sx[:, 0] = 1.0
    st_sq = np.zeros((P, 2), dtype=f32); st_sq[:, 1] = 1.0
    shared["c_stat_sx"] = st_sx
    shared["c_stat_sq"] = st_sq
    masks = np.zeros((4, P, 512), dtype=f32)
    for i in range(4):
        kk = 128 * i + np.arange(P)[:, None]
        qq = np.arange(512)[None, :]
        masks[i] = (kk <= qq).astype(f32)
    shared["c_diagmask"] = masks
    i = np.arange(DM // 2, dtype=np.float64)
    theta = 10000.0 ** (-2.0 * (i - 1.0) / DM)
    ang = np.arange(LSEQ, dtype=np.float64)[:, None] * theta[None, :]
    shared["c_rope_cos"] = np.ascontiguousarray(np.cos(ang).T, dtype=f32)
    shared["c_rope_sin"] = np.ascontiguousarray(np.sin(ang).T, dtype=f32)
    shared["c_iota8"] = np.tile(np.arange(8, dtype=f32), (P, 1))
    rs32 = np.zeros((32, 32 * P), dtype=f32)
    for r in range(32):
        rs32[r, r * P:(r + 1) * P] = 1.0
    shared["c_rowsel32"] = rs32
    rs16 = np.zeros((16, 16 * P), dtype=f32)
    for r in range(16):
        rs16[r, r * P:(r + 1) * P] = 1.0
    shared["c_rowsel16"] = rs16
    rs2 = np.zeros((2, 2 * P), dtype=f32)
    rs2[0, 0:P] = 1.0
    rs2[1, P:2 * P] = 1.0
    shared["c_rowsel2"] = rs2
    shared["c_zero_row"] = np.zeros((1, DM), dtype=f32)

    shared["w_in"] = inp["L0_attn_w_in"]
    shared["norm_w"] = inp["L0_attn_norm_w"].reshape(8, P).T.copy()
    convw = inp["L0_attn_conv_w"][:, 0, :]
    diag = np.zeros((4, 16, P, P), dtype=f32)
    for k in range(4):
        for dc in range(16):
            np.fill_diagonal(diag[k, dc], convw[dc * P:(dc + 1) * P, k])
    shared["conv_diag"] = diag
    shared["conv_b"] = inp["L0_attn_conv_b"].reshape(16, P).T.copy()
    shared["w_x"] = inp["L0_attn_w_x"]
    shared["w_dt"] = inp["L0_attn_w_dt"]
    shared["b_dt"] = inp["L0_attn_b_dt"].reshape(16, P).T.copy()
    al = inp["L0_attn_A_log"].reshape(16, P, NST)
    shared["A_log"] = np.ascontiguousarray(al.transpose(1, 0, 2).reshape(P, 256))
    shared["Dp"] = inp["L0_attn_D"].reshape(16, P).T.copy()
    shared["w_out"] = inp["L0_attn_w_out"]

    AsR, BsR = _expand_kron(inp["L0_ffn_G1_R"], inp["L0_ffn_G2_R"])
    AsI, BsI = _expand_kron(inp["L0_ffn_G1_I"], inp["L0_ffn_G2_I"])
    shared["kron_AR"] = np.stack(AsR); shared["kron_BR"] = np.stack(BsR)
    shared["kron_AI"] = np.stack(AsI); shared["kron_BI"] = np.stack(BsI)
    shared["ffn_bias"] = np.concatenate([inp["L0_ffn_bias_real"], inp["L0_ffn_bias_imag"]]).reshape(8, P).T.copy()

    perm = np.concatenate([np.arange(0, DM, 2), np.arange(1, DM, 2)])
    for nm in ["wq", "wk", "wv"]:
        w = inp[f"L1_attn_{nm}"]
        shared[nm + "_t"] = np.ascontiguousarray(w[perm, :].T)
    ipw = inp["L1_attn_in_proj_w"]
    ipb = inp["L1_attn_in_proj_b"]
    wqi, wki, wvi = np.split(ipw, 3, 0)
    shared["wqi_t"] = np.ascontiguousarray(wqi[:, perm].T)
    shared["wki_t"] = np.ascontiguousarray(wki[:, perm].T)
    shared["wvi_t"] = np.ascontiguousarray(wvi[:, perm].T)
    bqi, bki, bvi = np.split(ipb, 3, 0)
    shared["bqi"] = bqi.reshape(8, P).T.copy()
    shared["bki"] = bki.reshape(8, P).T.copy()
    shared["bvi"] = bvi.reshape(1, DM)
    shared["out_w_t"] = np.ascontiguousarray(inp["L1_attn_out_w"].T)
    shared["out_b"] = inp["L1_attn_out_b"].reshape(8, P).T.copy()

    shared["Wg"] = inp["L1_ffn_Wg"]
    shared["W1"] = inp["L1_ffn_W1"]
    shared["b1"] = np.ascontiguousarray(inp["L1_ffn_b1"].reshape(NEXP, 32, P).transpose(0, 2, 1))
    shared["W2"] = inp["L1_ffn_W2"]
    shared["b2"] = inp["L1_ffn_b2"]

    x = inp["L0_attn_x"]
    in_maps = []
    for c in range(NCORES):
        m = dict(shared)
        m["x_in"] = np.ascontiguousarray(x[2 * c:2 * c + 2])
        cm = np.zeros((8, 1), dtype=f32); cm[:c, 0] = 1.0
        m["c_coremask"] = cm
        in_maps.append(m)
    return in_maps


class Plan:
    def __init__(self):
        self.items = []

    def em(self, engine, fn):
        self.items.append((engine, fn))

    def replay(self, engine, eng):
        for e, fn in self.items:
            if e == engine:
                fn(eng)


class Ctr:
    def __init__(self, sem):
        self.sem = sem
        self.v = 0

    def inc(self, n=1):
        self.v += n
        return self.v


ENGS = ["sync", "tensor", "vector", "scalar", "gpsimd"]


def barrier(plan, ctr):
    target = ctr.inc(len(ENGS))
    for e in ENGS:
        def fn(eng, t=target):
            eng.nop().then_inc(ctr.sem, 1)
            eng.wait_ge(ctr.sem, t)
        plan.em(e, fn)


def build_nc(debug_outputs=(), stop_after=None):
    nc = bass.Bass("TRN2", target_bir_lowering=False)
    g = {}

    def din(name, shape, dt=F32):
        g[name] = nc.dram_tensor(name, list(shape), dt, kind="ExternalInput").ap()

    def dscr(name, shape, dt=F32):
        kind = "ExternalOutput" if name in debug_outputs else "Internal"
        g[name] = nc.dram_tensor(name, list(shape), dt, kind=kind).ap()

    din("x_in", [NSEQ, LSEQ, DM])
    din("c_ident_b", [P, P]); din("c_ident_f", [P, P]); din("c_ones_row", [1, P])
    din("c_lt_strict", [P, P]); din("c_stat_sx", [P, 2]); din("c_stat_sq", [P, 2])
    din("c_ones_col", [P, 1])
    din("c_diagmask", [4, P, 512]); din("c_rope_cos", [512, LSEQ]); din("c_rope_sin", [512, LSEQ])
    din("c_iota8", [P, 8]); din("c_coremask", [8, 1]); din("c_zero_row", [1, DM])
    din("c_rowsel32", [32, 32 * P]); din("c_rowsel16", [16, 16 * P]); din("c_rowsel2", [2, 2 * P])
    din("w_in", [DM, 2 * DI]); din("norm_w", [P, 8])
    din("conv_diag", [4, 16, P, P]); din("conv_b", [P, 16])
    din("w_x", [DI, 96]); din("w_dt", [DTR, DI]); din("b_dt", [P, 16])
    din("A_log", [P, 256]); din("Dp", [P, 16]); din("w_out", [DI, DM])
    din("kron_AR", [4, 512, 512]); din("kron_BR", [4, 512, 512])
    din("kron_AI", [4, 512, 512]); din("kron_BI", [4, 512, 512]); din("ffn_bias", [P, 8])
    din("wq_t", [DM, DM]); din("wk_t", [DM, DM]); din("wv_t", [DM, DM])
    din("wqi_t", [DM, DM]); din("wki_t", [DM, DM]); din("wvi_t", [DM, DM])
    din("bqi", [P, 8]); din("bki", [P, 8]); din("bvi", [1, DM])
    din("out_w_t", [DM, DM]); din("out_b", [P, 8])
    din("Wg", [DM, NEXP]); din("W1", [NEXP, DM, 4 * DM]); din("b1", [NEXP, P, 32])
    din("W2", [NEXP, 4 * DM, DM]); din("b2", [NEXP, DM])

    g["out"] = nc.dram_tensor("out", [NSEQ, LSEQ, DM], F32, kind="ExternalOutput").ap()

    dscr("xs", [DM, TPC])
    dscr("sc_u", [DI, TPC], BF16)
    dscr("sc_delta", [DI, TPC])
    dscr("sc_du", [DI, TPC])
    dscr("sc_gate", [DI, TPC], BF16)
    dscr("sc_q", [DM, TPC], BF16)
    dscr("sc_k", [DM, TPC], BF16)
    dscr("sc_v", [DM, TPC], BF16)
    dscr("sc_qh", [DM, TPC], BF16)
    dscr("sc_kh", [DM, TPC], BF16)
    dscr("sc_vh_tm", [TPC, DM], BF16)
    dscr("sc_o", [DM, TPC], BF16)
    dscr("sc_xn4_tm", [TPC + 1, DM], BF16)
    dscr("sc_x3_tm", [TPC, DM])
    dscr("sc_dbg", [P, 2048])
    dscr("sc_dbg2", [2, TPC])
    dscr("sc_gh", [4 * DM, CAPL], BF16)
    g["agin"] = nc.dram_tensor("agin", [1, NEXP], F32, kind="Internal").ap()
    g["agout"] = nc.dram_tensor("agout", [NEXP, NEXP], F32, kind="Internal").ap()
    dscr("sc_ein", [NEXP * CAPL + 1, DM], BF16)
    dscr("sc_eout", [NEXP * CAPL + 1, DM], BF16)

    plan = Plan()
    ctx = ExitStack()

    sem_handles = {}
    for nm in ["bar", "s_ld", "s_lg", "s_mm", "s_ve", "s_ac", "s_gp", "s_st", "s_x1", "s_x2"]:
        sem_handles[nm] = ctx.enter_context(nc.semaphore(nm))
    C = {nm: Ctr(h) for nm, h in sem_handles.items()}

    st = _State(nc, g, plan, C, ctx)
    st.emit_all(stop_after=stop_after)

    with nc.Block() as block:
        @block.sync
        def _(e): plan.replay("sync", e)
        @block.tensor
        def _(e): plan.replay("tensor", e)
        @block.vector
        def _(e): plan.replay("vector", e)
        @block.scalar
        def _(e): plan.replay("scalar", e)
        @block.gpsimd
        def _(e): plan.replay("gpsimd", e)

    ctx.close()
    return nc


class _State:

    def __init__(self, nc, g, plan, C, ctx):
        self.nc = nc
        self.g = g
        self.plan = plan
        self.C = C
        self.ctx = ctx
        self.consts = {}

    def sb(self, name, shape, dt=F32):
        return self.ctx.enter_context(self.nc.sbuf_tensor(name, list(shape), dt))

    def ps(self, name, shape, dt=F32):
        return self.ctx.enter_context(self.nc.psum_tensor(name, list(shape), dt))

    def load(self, dst_ap, src_ap, cast=False, wait=()):
        name = "s_lg" if cast else "s_ld"
        C = self.C[name]
        t = C.inc(16)
        eng = "gpsimd" if cast else "sync"
        def fn(e, d=dst_ap, s=src_ap, w=tuple(wait), p=t - 16):
            e.wait_ge(C.sem, p)
            for sem, v in w:
                e.wait_ge(sem, v)
            e.dma_start(out=d, in_=s).then_inc(C.sem, 16)
        self.plan.em(eng, fn)
        return (name, t)

    def store(self, dst_ap, src_ap, wait=()):
        C = self.C["s_st"]
        t = C.inc(16)
        def fn(e, d=dst_ap, s=src_ap, w=tuple(wait), p=t - 16):
            e.wait_ge(C.sem, p)
            for sem, v in w:
                e.wait_ge(sem, v)
            e.dma_start(out=d, in_=s).then_inc(C.sem, 16)
        self.plan.em("sync", fn)
        return t

    def wait(self, engine, ctr_name, val):
        if isinstance(ctr_name, tuple):
            ctr_name, val = ctr_name
        elif isinstance(val, tuple):
            ctr_name, val = val
        sem = self.C[ctr_name].sem
        self.plan.em(engine, lambda e, s=sem, v=val: e.wait_ge(s, v))

    def emit(self, engine, fn, inc=None):
        if inc is None:
            self.plan.em(engine, fn)
            return None
        C = self.C[inc]
        t = C.inc(1)
        def wrapped(e, f=fn, s=C.sem):
            f(e).then_inc(s, 1)
        self.plan.em(engine, wrapped)
        return t

    def barrier(self):
        barrier(self.plan, self.C["bar"])

    def emit_all(self, stop_after=None):
        phases = [self.p_consts, self.p0_transpose_in, self.p1_mamba, self.p2_ffn,
                  self.p3_attn, self.p4_moe]
        names = ["pc", "p0", "p1", "p2", "p3", "p4"]
        for nm, ph in zip(names, phases):
            ph()
            self.barrier()
            if stop_after == nm:
                break
        stv = self.C["s_st"].v
        ssem = self.C["s_st"].sem
        self.plan.em("sync", lambda e, v=stv, s=ssem: e.wait_ge(s, v))

    def p_consts(self):
        g = self.g
        cs = self.consts
        cs["ident_b"] = self.sb("ident_b", [P, P], BF16)
        cs["ident_f"] = self.sb("ident_f", [P, P], F32)
        cs["ones_row_b"] = self.sb("ones_row_b", [1, P], BF16)
        cs["ones_row_f"] = self.sb("ones_row_f", [1, P], F32)
        cs["ones_col_f"] = self.sb("ones_col_f", [P, 1], F32)
        cs["lt_b"] = self.sb("lt_b", [P, P], BF16)
        cs["stat_sx"] = self.sb("stat_sx", [P, 2], BF16)
        cs["stat_sq"] = self.sb("stat_sq", [P, 2], BF16)
        cs["diagmask"] = self.sb("diagmask", [P, 4 * 512], BF16)
        cs["iota8"] = self.sb("iota8", [P, 8], F32)
        cs["rowsel32_b"] = self.sb("rowsel32_b", [32, 32 * P], BF16)
        cs["rowsel16_f"] = self.sb("rowsel16_f", [16, 16 * P], F32)
        cs["rowsel2_f"] = self.sb("rowsel2_f", [2, 2 * P], F32)
        cs["coremask"] = self.sb("coremask", [8, 1], F32)
        self.load(cs["ident_b"][:, :], g["c_ident_b"][:, :], cast=True)
        self.load(cs["ident_f"][:, :], g["c_ident_f"][:, :])
        self.load(cs["ones_row_b"][:, :], g["c_ones_row"][:, :], cast=True)
        self.load(cs["ones_row_f"][:, :], g["c_ones_row"][:, :])
        self.load(cs["ones_col_f"][:, :], g["c_ones_col"][:, :])
        self.load(cs["lt_b"][:, :], g["c_lt_strict"][:, :], cast=True)
        self.load(cs["stat_sx"][:, :], g["c_stat_sx"][:, :], cast=True)
        self.load(cs["stat_sq"][:, :], g["c_stat_sq"][:, :], cast=True)
        for i in range(4):
            self.load(cs["diagmask"][:, 512 * i:512 * (i + 1)], g["c_diagmask"][i, :, :], cast=True)
        self.load(cs["iota8"][:, :], g["c_iota8"][:, :])
        self.load(cs["rowsel32_b"][:, :], g["c_rowsel32"][:, :], cast=True)
        self.load(cs["rowsel16_f"][:, :], g["c_rowsel16"][:, :])
        self.load(cs["rowsel2_f"][:, :], g["c_rowsel2"][:, :])
        self.load(cs["coremask"][:, :], g["c_coremask"][:, :])

    def p0_transpose_in(self):
        g, nc = self.g, self.nc
        with ExitStack() as px:
            tin = px.enter_context(nc.sbuf_tensor("p0_tin", [P, P], F32))
            tev = px.enter_context(nc.sbuf_tensor("p0_tev", [P, 16 * P], F32))
            pst = px.enter_context(nc.psum_tensor("p0_ps", [P, P], F32))
            for seq in range(NSEQ):
                for dc in range(8):
                    for tt in range(16):
                        lv = self.load(tin[:, :], g["x_in"][seq, tt * P:(tt + 1) * P, dc * P:(dc + 1) * P],
                                       wait=[(self.C["s_mm"].sem, self.C["s_mm"].v)])
                        self.wait("tensor", "s_ld", lv)
                        mv = self.emit("tensor", lambda e, o=pst[:, :], i=tin[:, :]:
                                       e.transpose(o, i, self.consts["ident_f"][:, :]), inc="s_mm")
                        self.wait("vector", "s_mm", mv)
                        vv = self.emit("vector", lambda e, o=tev[:, tt * P:(tt + 1) * P], i=pst[:, :]:
                                       e.tensor_copy(o, i), inc="s_ve")
                        self.wait("tensor", "s_ve", vv)
                    sv = self.C["s_ve"].v
                    self.store(g["xs"][dc * P:(dc + 1) * P, seq * LSEQ:(seq + 1) * LSEQ], tev[:, :],
                               wait=[(self.C["s_ve"].sem, sv)])
                    self.wait("vector", "s_st", self.C["s_st"].v)

    def mm(self, ps_ap, stat_ap, mov_ap, start, stop, inc=None):
        def fn(e, o=ps_ap, s=stat_ap, m=mov_ap, st=start, sp=stop):
            return e.matmul(o, s, m, start=st, stop=sp)
        return self.emit("tensor", fn, inc=inc)

    def act(self, out, in_, func, scale=None, bias=None, inc=None, engine="scalar"):
        kw = {}
        if scale is not None:
            kw["scale"] = scale
        if bias is not None:
            kw["bias"] = bias
        def fn(e, o=out, i=in_, f=func, k=kw):
            return e.activation(out=o, in_=i, func=f, **k)
        return self.emit(engine, fn, inc=inc)

    def tt(self, out, a, b, op, inc=None, engine="vector"):
        def fn(e, o=out, x=a, y=b, p=op):
            return e.tensor_tensor(o, x, y, op=p)
        t = self.emit(engine, fn, inc="s_ve")
        self.wait(engine, "s_ve", t)
        return t

    def stt(self, out, in0, scalar, in1, op0, op1, inc=None, engine="vector"):
        def fn(e, o=out, i0=in0, s=scalar, i1=in1, p0=op0, p1=op1):
            return e.scalar_tensor_tensor(o, i0, s, i1, op0=p0, op1=p1)
        t = self.emit(engine, fn, inc="s_ve")
        self.wait(engine, "s_ve", t)
        return t

    def tscal(self, out, in_, s1, s2, op0, op1=None, inc=None, engine="vector"):
        def fn(e, o=out, i=in_, a=s1, b=s2, p0=op0, p1=op1):
            if p1 is None:
                return e.tensor_scalar(o, i, a, None, op0=p0)
            return e.tensor_scalar(o, i, a, b, op0=p0, op1=p1)
        t = self.emit(engine, fn, inc="s_ve")
        self.wait(engine, "s_ve", t)
        return t

    def vcopy(self, out, in_, inc=None, engine="vector"):
        t = self.emit(engine, lambda e, o=out, i=in_: e.tensor_copy(o, i), inc="s_ve")
        self.wait(engine, "s_ve", t)
        return t

    def vmemset(self, ap, val, inc=None, engine="vector"):
        t = self.emit(engine, lambda e, a=ap, v=val: e.memset(a, v), inc="s_ve")
        self.wait(engine, "s_ve", t)
        return t

    def scan_op(self, out, a, b, initial, inc=None):
        def fn(e, o=out, x=a, y=b, i=initial):
            return e.tensor_tensor_scan(o, x, y, i, op0=OP.mult, op1=OP.add)
        t = self.emit("vector", fn, inc="s_ve")
        self.wait("vector", "s_ve", t)
        return t

    def recip(self, out, in_, inc=None):
        t = self.emit("vector", lambda e, o=out, i=in_: e.reciprocal(o, i), inc="s_ve")
        self.wait("vector", "s_ve", t)
        return t

    def vreduce(self, out, in_, op=OP.add):
        t = self.emit("vector", lambda e, o=out, i=in_, p=op:
                      e.tensor_reduce(o, i, axis=mybir.AxisListType.X, op=p), inc="s_ve")
        self.wait("vector", "s_ve", t)
        return t

    def transpose_mm(self, ps_ap, in_ap, dtype_b, inc=None):
        ident = self.consts["ident_b" if dtype_b else "ident_f"]
        def fn(e, o=ps_ap, i=in_ap, idn=ident[:, :]):
            return e.transpose(o, i, idn)
        return self.emit("tensor", fn, inc=inc)

    def rep_row(self, ps_ap, row_ap, inc=None):
        return self.mm(ps_ap, self.consts["ones_row_f"][:, :], row_ap, True, True, inc=inc)

    def rep_stat(self, ps_ap, block16_ap, r, inc=None):
        sel = self.consts["rowsel16_f"][:, r * P:(r + 1) * P]
        return self.mm(ps_ap, sel, block16_ap, True, True, inc=inc)

    def rep_row32(self, ps_ap, block32_ap, r, inc=None):
        sel = self.consts["rowsel32_b"][:, r * P:(r + 1) * P]
        return self.mm(ps_ap, sel, block32_ap, True, True, inc=inc)

    def ln_chain(self, stats_ap, c0, w, mode):
        sel = self.consts["rowsel2_f"]
        self.mm(w["ps_m"][:, :], sel[:, 0:P], stats_ap[0:2, c0:c0 + 512], True, True)
        mv = self.mm(w["ps_e"][:, :], sel[:, P:2 * P], stats_ap[0:2, c0:c0 + 512], True, True, inc="s_mm")
        self.wait("vector", "s_mm", mv)
        self.tscal(w["m"][:, :], w["ps_m"][:, :], 1.0 / DM, None, OP.mult)
        self.tscal(w["t1"][:, :], w["ps_e"][:, :], 1.0 / DM, None, OP.mult)
        self.tt(w["v"][:, :], w["m"][:, :], w["m"][:, :], OP.mult)
        self.tt(w["v"][:, :], w["t1"][:, :], w["v"][:, :], OP.subtract)
        vv = self.tscal(w["t1"][:, :], w["v"][:, :], 1e-5, None, OP.add, inc="s_ve")
        self.wait("scalar", "s_ve", vv)
        av = self.act(w["t1"][:, :], w["t1"][:, :], AF.Sqrt, inc="s_ac")
        self.wait("vector", "s_ac", av)
        self.recip(w["t1"][:, :], w["t1"][:, :])
        if mode == "rms":
            self.tt(w["t2"][:, :], w["v"][:, :], w["t1"][:, :], OP.mult)
            self.tt(w["t2"][:, :], w["t2"][:, :], w["t1"][:, :], OP.mult)
            vv = self.tscal(w["t2"][:, :], w["t2"][:, :], EPS32, None, OP.add, inc="s_ve")
            self.wait("scalar", "s_ve", vv)
            av = self.act(w["t2"][:, :], w["t2"][:, :], AF.Sqrt, inc="s_ac")
            self.wait("vector", "s_ac", av)
            self.recip(w["t2"][:, :], w["t2"][:, :])
            self.tt(w["o1"][:, :], w["t1"][:, :], w["t2"][:, :], OP.mult)
            vv = self.tt(w["o2"][:, :], w["m"][:, :], w["o1"][:, :], OP.mult, inc="s_ve")
        elif mode == "ln":
            self.vcopy(w["o1"][:, :], w["t1"][:, :])
            vv = self.tt(w["o2"][:, :], w["m"][:, :], w["t1"][:, :], OP.mult, inc="s_ve")
        else:
            self.tscal(w["o1"][:, :], w["t1"][:, :], 1.0, None, OP.add)
            vv = self.tt(w["o2"][:, :], w["m"][:, :], w["t1"][:, :], OP.mult, inc="s_ve")
        return vv

    def stats_pass(self, x_t, xq_t, ps_st):
        g, C = self.g, self.C
        for tc in range(TPC // 512):
            for dc in range(8):
                sl = (dc % 2) * 512
                lvx = self.load(x_t[:, sl:sl + 512],
                                g["xs"][dc * P:(dc + 1) * P, tc * 512:(tc + 1) * 512],
                                cast=True, wait=[(C["s_mm"].sem, C["s_mm"].v)])
                self.wait("scalar", "s_ld", lvx)
                avq = self.act(xq_t[:, :], x_t[:, sl:sl + 512], AF.Square, inc="s_ac")
                self.wait("tensor", "s_ac", avq)
                self.wait("tensor", "s_ld", lvx)
                self.mm(ps_st[0:2, 0:512], self.consts["stat_sx"][:, :], x_t[:, sl:sl + 512], dc == 0, False)
                mmv = self.mm(ps_st[0:2, 0:512], self.consts["stat_sq"][:, :], xq_t[:, :], False, dc == 7, inc="s_mm")
                self.wait("scalar", "s_mm", mmv)
            self.wait("vector", "s_mm", mmv)
            vv = self.vcopy(self.stats[0:2, tc * 512:(tc + 1) * 512], ps_st[0:2, 0:512], inc="s_ve")
            self.wait("tensor", "s_ve", vv)
        return vv

    def p1_mamba(self):
        g, nc, C = self.g, self.nc, self.C

        normw = self.sb("normw", [P, 8])
        convb = self.sb("convb", [P, 16])
        nb2 = self.sb("nb2", [P, 16])
        bdt = self.sb("bdt", [P, 16])
        Asc = self.sb("Asc", [P, 256])
        Dp = self.sb("Dp_sb", [P, 16])
        stats = self.sb("stats", [2, TPC])
        self.stats = stats
        BCb = self.sb("BCb", [32, TPC], BF16)
        hstate = self.sb("hstate", [P, 256 * NSEQ])

        lv = self.load(normw[:, :], g["norm_w"][:, :])
        lv = self.load(convb[:, :], g["conv_b"][:, :])
        lv = self.load(bdt[:, :], g["b_dt"][:, :])
        lv = self.load(Dp[:, :], g["Dp"][:, :])
        alog_t = self.sb("alog_t", [P, 256])
        lv = self.load(alog_t[:, :], g["A_log"][:, :])
        self.wait("vector", "s_ld", lv)
        vv = self.tscal(nb2[:, :], bdt[:, :], -2.0, None, OP.mult, inc="s_ve")
        self.wait("scalar", "s_ve", vv)
        self.wait("scalar", "s_ld", lv)
        av = self.act(Asc[:, :], alog_t[:, :], AF.Exp, inc="s_ac")
        self.wait("vector", "s_ac", av)
        vv = self.tscal(Asc[:, :], Asc[:, :], -1.0, None, OP.mult, inc="s_ve")
        self.wait("scalar", "s_ve", vv)

        with ExitStack() as px:
            sbl = lambda n, s, d=F32: px.enter_context(nc.sbuf_tensor(n, list(s), d))
            w_in_bf = sbl("w_in_bf", [P, 8 * 4096], BF16)
            w_x_bf = sbl("w_x_bf", [P, 16 * 96], BF16)
            w_dt_bf = sbl("w_dt_bf", [DTR, DI], BF16)
            cdiag = sbl("cdiag", [P, 4 * 16 * P], BF16)
            x_t = sbl("x_t", [P, 2 * 512], BF16)
            xq_t = sbl("xq_t", [P, 512], BF16)
            lnw = {}
            for nm in ["m", "v", "t1", "t2", "o1", "o2"]:
                lnw[nm] = sbl("lnw_" + nm, [P, 512])
            xrb = sbl("xrb", [P, 8 * 512], BF16)
            xssm_ext = sbl("xssm_ext", [P, 16 * 515], BF16)
            u_all = sbl("u_all", [P, 16 * 512], BF16)
            dtl_t = sbl("dtl_t", [DTR, 512], BF16)
            sg_t = sbl("sg_t", [P, 512])
            l_t = sbl("l_t", [P, 512])
            dl_t = sbl("dl_t", [P, 512])
            du_t = sbl("du_t", [P, 512])
            ps_xp = px.enter_context(nc.psum_tensor("ps_xp", [P, 2 * 512], F32))
            ps_cv = px.enter_context(nc.psum_tensor("ps_cv", [P, 512], F32))
            ps_db = px.enter_context(nc.psum_tensor("ps_db", [DTR, 512], F32))
            ps_bc = px.enter_context(nc.psum_tensor("ps_bc", [32, 512], F32))
            ps_dt = px.enter_context(nc.psum_tensor("ps_dt", [P, 512], F32))
            ps_r1 = px.enter_context(nc.psum_tensor("ps_r1", [P, 512], F32))
            ps_r2 = px.enter_context(nc.psum_tensor("ps_r2", [P, 512], F32))
            ps_st = ps_r1

            for dc in range(8):
                self.load(w_in_bf[:, dc * 4096:(dc + 1) * 4096], g["w_in"][dc * P:(dc + 1) * P, :], cast=True)
            for dc in range(16):
                self.load(w_x_bf[:, dc * 96:(dc + 1) * 96], g["w_x"][dc * P:(dc + 1) * P, :], cast=True)
            lv = self.load(w_dt_bf[:, :], g["w_dt"][:, :], cast=True)
            for k in range(4):
                for dc in range(16):
                    lv = self.load(cdiag[:, (k * 16 + dc) * P:(k * 16 + dc + 1) * P],
                                   g["conv_diag"][k, dc, :, :], cast=True)
            wloads = C["s_lg"].v
            self.wait("vector", "s_lg", wloads)
            for dc in range(8):
                vv = self.tt(w_in_bf[:, dc * 4096:(dc + 1) * 4096],
                             w_in_bf[:, dc * 4096:(dc + 1) * 4096],
                             normw[:, dc:dc + 1].to_broadcast([P, 4096]), OP.mult, inc="s_ve")
            w_ready_v = vv

            self.stats_pass(x_t, xq_t, ps_st)

            for seq in range(NSEQ):
                for tcl in range(4):
                    tc = seq * 4 + tcl
                    c0 = tc * 512
                    wln = dict(lnw); wln["ps_m"] = ps_r1; wln["ps_e"] = ps_r2
                    chv = self.ln_chain(stats, c0, wln, "rms")
                    self.wait("scalar", "s_ve", C["s_ve"].v)
                    for dc in range(8):
                        sl = (dc % 2) * 512
                        lvx = self.load(x_t[:, sl:sl + 512],
                                        g["xs"][dc * P:(dc + 1) * P, c0:c0 + 512],
                                        cast=True,
                                        wait=[(C["s_ve"].sem, chv if dc < 2 else vv)])
                        self.wait("vector", "s_ld", lvx)
                        self.tt(xrb[:, dc * 512:(dc + 1) * 512], x_t[:, sl:sl + 512],
                                lnw["o1"][:, :], OP.mult)
                        vv = self.tt(xrb[:, dc * 512:(dc + 1) * 512], xrb[:, dc * 512:(dc + 1) * 512],
                                     lnw["o2"][:, :], OP.subtract, inc="s_ve")
                    self.wait("tensor", "s_ve", vv)
                    if tc == 0:
                        self.wait("tensor", "s_ve", w_ready_v)

                    if tcl == 0:
                        for oc in range(16):
                            self.vmemset(xssm_ext[:, oc * 515:oc * 515 + 3], 0.0, engine="gpsimd")
                    else:
                        for oc in range(16):
                            self.vcopy(xssm_ext[:, oc * 515:oc * 515 + 3],
                                       xssm_ext[:, oc * 515 + 512:oc * 515 + 515], engine="gpsimd")
                    gv_tail = self.emit("gpsimd", lambda e: e.engine_nop(), inc="s_gp")

                    for oc in range(32):
                        pxp = ps_xp[:, (oc % 2) * 512:(oc % 2) * 512 + 512]
                        for dc in range(8):
                            stat = w_in_bf[:, dc * 4096 + oc * P:dc * 4096 + (oc + 1) * P]
                            mv = self.mm(pxp, stat, xrb[:, dc * 512:(dc + 1) * 512], dc == 0, dc == 7,
                                         inc=("s_mm" if dc == 7 else None))
                        if oc < 16:
                            if oc == 0:
                                self.wait("scalar", "s_gp", gv_tail)
                            self.wait("scalar", "s_mm", mv)
                            av = self.act(xssm_ext[:, oc * 515 + 3:oc * 515 + 515], pxp, AF.Copy, inc="s_ac")
                        else:
                            self.wait("scalar", "s_mm", mv)
                            av = self.act(u_all[:, (oc - 16) * 512:(oc - 15) * 512], pxp, AF.Silu, inc="s_ac")
                            self.store(g["sc_gate"][(oc - 16) * P:(oc - 15) * P, c0:c0 + 512],
                                       u_all[:, (oc - 16) * 512:(oc - 15) * 512],
                                       wait=[(C["s_ac"].sem, av)])
                        self.wait("tensor", "s_ac", av)
                    self.wait("scalar", "s_st", C["s_st"].v)

                    for oc in range(16):
                        for k in range(4):
                            stat = cdiag[:, (k * 16 + oc) * P:(k * 16 + oc + 1) * P]
                            mov = xssm_ext[:, oc * 515 + k:oc * 515 + k + 512]
                            mv = self.mm(ps_cv[:, :], stat, mov, k == 0, k == 3,
                                         inc=("s_mm" if k == 3 else None))
                        self.wait("scalar", "s_mm", mv)
                        av = self.act(u_all[:, oc * 512:(oc + 1) * 512], ps_cv[:, :], AF.Silu,
                                      bias=convb[:, oc:oc + 1], inc="s_ac")
                        self.store(g["sc_u"][oc * P:(oc + 1) * P, c0:c0 + 512],
                                   u_all[:, oc * 512:(oc + 1) * 512], wait=[(C["s_ac"].sem, av)])
                        self.wait("tensor", "s_ac", av)

                    for dc in range(16):
                        self.mm(ps_db[:, :], w_x_bf[:, dc * 96:dc * 96 + DTR],
                                u_all[:, dc * 512:(dc + 1) * 512], dc == 0, dc == 15)
                        mv = self.mm(ps_bc[:, :], w_x_bf[:, dc * 96 + DTR:(dc + 1) * 96],
                                     u_all[:, dc * 512:(dc + 1) * 512], dc == 0, dc == 15,
                                     inc=("s_mm" if dc == 15 else None))
                    self.wait("vector", "s_mm", mv)
                    self.vcopy(dtl_t[:, :], ps_db[0:DTR, :])
                    vv = self.vcopy(BCb[0:32, c0:c0 + 512], ps_bc[0:32, :], inc="s_ve")
                    self.wait("tensor", "s_ve", vv)

                    for oc in range(16):
                        mv = self.mm(ps_dt[:, :], w_dt_bf[:, oc * P:(oc + 1) * P], dtl_t[:, :],
                                     True, True, inc="s_mm")
                        self.wait("scalar", "s_mm", mv)
                        self.act(sg_t[:, :], ps_dt[:, :], AF.Sigmoid, scale=-1.0, bias=nb2[:, oc:oc + 1])
                        av = self.act(l_t[:, :], sg_t[:, :], AF.Ln, inc="s_ac")
                        self.wait("tensor", "s_ac", av)
                        self.wait("vector", "s_ac", av)
                        self.tscal(dl_t[:, :], l_t[:, :], -1.0, None, OP.mult)
                        vv = self.tt(du_t[:, :], dl_t[:, :], u_all[:, oc * 512:(oc + 1) * 512],
                                     OP.mult, inc="s_ve")
                        self.store(g["sc_delta"][oc * P:(oc + 1) * P, c0:c0 + 512], dl_t[:, :],
                                   wait=[(C["s_ve"].sem, vv)])
                        self.store(g["sc_du"][oc * P:(oc + 1) * P, c0:c0 + 512], du_t[:, :],
                                   wait=[(C["s_ve"].sem, vv)])
                        self.wait("vector", "s_st", C["s_st"].v)

        self.barrier()
        with ExitStack() as px:
            sbl = lambda n, s, d=F32: px.enter_context(nc.sbuf_tensor(n, list(s), d))
            w_out_bf = sbl("w_out_bf", [P, 16 * DM], BF16)
            Brep = sbl("Brep", [P, 16 * 1024], BF16)
            Crep = sbl("Crep", [P, 16 * 1024], BF16)
            delta_t = sbl("delta_t", [P, 1024], BF16)
            duu_t = sbl("duu_t", [P, 1024], BF16)
            arena = sbl("p1e_arena", [P, 3584])
            dA_t = arena[:, 0:1024]
            beta_t = arena[:, 1024:2048]
            h_t = sbl("h_t", [P, 1024], BF16)
            ht_t = sbl("ht_t", [P, 1024], BF16)
            u_t = sbl("u_t", [P, 1024], BF16)
            gt_t = sbl("gt_t", [P, 1024], BF16)

            xf_all = sbl("xf_all", [P, 16 * 1024], BF16)
            ys_t = arena[:, 3072:3584].bitcast(BF16)
            x_t2 = arena[:, 3072:3584]
            lnw2 = {nm: arena[:, i * 512:(i + 1) * 512]
                    for i, nm in enumerate(["m", "v", "t1", "t2", "o1", "o2"])}
            ps_rep = px.enter_context(nc.psum_tensor("ps_rep", [P, 512], F32))
            ps_y = px.enter_context(nc.psum_tensor("ps_y", [P, 1024], F32))
            ps_wo = px.enter_context(nc.psum_tensor("ps_wo", [P, 512], F32))
            ps_ra = px.enter_context(nc.psum_tensor("ps_ra", [P, 512], F32))
            ps_rb = px.enter_context(nc.psum_tensor("ps_rb", [P, 512], F32))

            for dc in range(16):
                lv = self.load(w_out_bf[:, dc * DM:(dc + 1) * DM], g["w_out"][dc * P:(dc + 1) * P, :], cast=True)
            self.wait("tensor", "s_ld", lv)

            for seq in range(NSEQ):
                for th in range(2):
                    hb0 = seq * LSEQ + th * 1024
                    for n in range(NST):
                        for j in range(2):
                            mv = self.rep_row32(ps_rep[:, :],
                                                BCb[0:32, hb0 + j * 512:hb0 + (j + 1) * 512],
                                                n, inc="s_mm")
                            self.wait("vector", "s_mm", mv)
                            vv = self.vcopy(Brep[:, n * 1024 + j * 512:n * 1024 + (j + 1) * 512],
                                            ps_rep[:, :], inc="s_ve")
                            self.wait("tensor", "s_ve", vv)
                            mv = self.rep_row32(ps_rep[:, :],
                                                BCb[0:32, hb0 + j * 512:hb0 + (j + 1) * 512],
                                                16 + n, inc="s_mm")
                            self.wait("vector", "s_mm", mv)
                            vv = self.vcopy(Crep[:, n * 1024 + j * 512:n * 1024 + (j + 1) * 512],
                                            ps_rep[:, :], inc="s_ve")
                            self.wait("tensor", "s_ve", vv)

                    for dk in range(16):
                        self.wait("gpsimd", "s_ve", C["s_ve"].v)
                        lv = self.load(delta_t[:, :], g["sc_delta"][dk * P:(dk + 1) * P, hb0:hb0 + 1024],
                                       cast=True)
                        lv = self.load(duu_t[:, :], g["sc_du"][dk * P:(dk + 1) * P, hb0:hb0 + 1024],
                                       cast=True)
                        self.load(u_t[:, :], g["sc_u"][dk * P:(dk + 1) * P, hb0:hb0 + 1024],
                                  wait=[(C["s_ve"].sem, C["s_ve"].v)])
                        lvg = self.load(gt_t[:, :], g["sc_gate"][dk * P:(dk + 1) * P, hb0:hb0 + 1024])
                        self.wait("scalar", "s_ld", lv)
                        self.wait("vector", "s_ld", lvg)
                        for n in range(NST):
                            av = self.act(dA_t, delta_t[:, :], AF.Exp,
                                          scale=Asc[:, dk * 16 + n:dk * 16 + n + 1], inc="s_ac")
                            self.wait("vector", "s_ac", av)
                            self.tt(beta_t, duu_t[:, :],
                                    Brep[:, n * 1024:(n + 1) * 1024], OP.mult)
                            if th == 0:
                                ini = 0.0
                            else:
                                ini = hstate[:, seq * 256 + dk * 16 + n:seq * 256 + dk * 16 + n + 1]
                            scv = self.scan_op(h_t[:, :], dA_t, beta_t, ini, inc="s_ve")
                            if th == 0:
                                self.vcopy(hstate[:, seq * 256 + dk * 16 + n:seq * 256 + dk * 16 + n + 1],
                                           h_t[:, 1023:1024])
                            vv = self.tt(ht_t[:, :], h_t[:, :], Crep[:, n * 1024:(n + 1) * 1024],
                                         OP.mult, inc="s_ve")
                            self.wait("tensor", "s_ve", vv)

                            for j in range(2):
                                mv = self.mm(ps_y[:, j * 512:(j + 1) * 512], self.consts["ident_b"][:, :],
                                             ht_t[:, j * 512:(j + 1) * 512], n == 0, n == NST - 1,
                                             inc=("s_mm" if j == 1 else None))
                            if n < NST - 1:
                                self.wait("vector", "s_mm", mv)
                                self.wait("scalar", "s_ve", vv)
                        self.wait("vector", "s_mm", mv)
                        self.stt(ys_t, u_t[:, :], Dp[:, dk:dk + 1], ps_y[:, :],
                                 OP.mult, OP.add)
                        vv = self.tt(xf_all[:, dk * 1024:(dk + 1) * 1024], ys_t, gt_t[:, :],
                                     OP.mult, inc="s_ve")
                        self.wait("tensor", "s_ve", vv)


                    for j in range(2):
                        c0 = hb0 + j * 512
                        wln = dict(lnw2); wln["ps_m"] = ps_ra; wln["ps_e"] = ps_rb
                        chv = self.ln_chain(stats, c0, wln, "res")
                        for oc in range(8):
                            for dk in range(16):
                                stat = w_out_bf[:, dk * DM + oc * P:dk * DM + (oc + 1) * P]
                                mv = self.mm(ps_wo[:, :], stat, xf_all[:, dk * 1024 + j * 512:dk * 1024 + (j + 1) * 512],
                                             dk == 0, dk == 15, inc=("s_mm" if dk == 15 else None))
                            lvx = self.load(x_t2[:, :], g["xs"][oc * P:(oc + 1) * P, c0:c0 + 512],
                                            wait=[(C["s_ve"].sem, C["s_ve"].v)])
                            self.wait("vector", "s_mm", mv)
                            self.wait("vector", "s_ld", lvx)
                            self.tt(x_t2[:, :], x_t2[:, :], lnw2["o1"], OP.mult)
                            self.tt(x_t2[:, :], x_t2[:, :], lnw2["o2"], OP.subtract)
                            vv = self.tt(x_t2[:, :], x_t2[:, :], ps_wo[:, :], OP.add, inc="s_ve")
                            self.store(g["xs"][oc * P:(oc + 1) * P, c0:c0 + 512], x_t2[:, :],
                                       wait=[(C["s_ve"].sem, vv)])
                            self.wait("tensor", "s_ve", vv)
                            self.wait("vector", "s_st", C["s_st"].v)

    def p2_ffn(self):
        g, nc, C = self.g, self.nc, self.C
        ffnb = self.sb("ffnb", [P, 8])
        lvb = self.load(ffnb[:, :], g["ffn_bias"][:, :])
        with ExitStack() as px:
            sbl = lambda n, s, d=F32: px.enter_context(nc.sbuf_tensor(n, list(s), d))
            wrt = sbl("wrt", [P, 2048], BF16)
            wit = sbl("wit", [P, 2048], BF16)
            nwit = sbl("nwit", [P, 2048], BF16)
            ka = sbl("ka", [P, 512])
            kb = sbl("kb", [P, 512])
            kacc = sbl("kacc", [P, 512])
            x_t = sbl("p2x_t", [P, 2 * 512], BF16)
            xq_t = sbl("p2xq_t", [P, 512], BF16)
            xn2 = sbl("xn2", [P, 8 * 512], BF16)
            ev_t = sbl("ev_t", [P, 512])
            xres = sbl("xres", [P, 512])
            lnw = {}
            for nm in ["m", "v", "t1", "t2", "o1", "o2"]:
                lnw[nm] = sbl("p2lnw_" + nm, [P, 512])
            ps_st = px.enter_context(nc.psum_tensor("p2ps_st", [P, 512], F32))
            ps_e = px.enter_context(nc.psum_tensor("p2ps_e", [P, 512], F32))
            ps_o = px.enter_context(nc.psum_tensor("p2ps_o", [P, 2 * 512], F32))

            for name in ["R", "I"]:
                dst = wrt if name == "R" else wit
                for pc in range(4):
                    for c in range(4):
                        lv = self.load(ka[:, :], g["kron_A" + name][c, pc * P:(pc + 1) * P, :],
                                       wait=[(C["s_ve"].sem, C["s_ve"].v)])
                        lv = self.load(kb[:, :], g["kron_B" + name][c, pc * P:(pc + 1) * P, :])
                        self.wait("vector", "s_ld", lv)
                        if c == 0:
                            vv = self.tt(kacc[:, :], ka[:, :], kb[:, :], OP.mult, inc="s_ve")
                        else:
                            self.tt(ka[:, :], ka[:, :], kb[:, :], OP.mult)
                            vv = self.tt(kacc[:, :], kacc[:, :], ka[:, :], OP.add, inc="s_ve")
                    vv = self.vcopy(dst[:, pc * 512:(pc + 1) * 512], kacc[:, :], inc="s_ve")
            vv = self.tscal(nwit[:, :], wit[:, :], -1.0, None, OP.mult, inc="s_ve")
            self.wait("tensor", "s_ve", vv)

            self.stats_pass(x_t, xq_t, ps_st)
            self.wait("scalar", "s_ld", lvb)

            wln = dict(lnw); wln["ps_m"] = ps_st; wln["ps_e"] = ps_e
            vv = C["s_ve"].v
            for tc in range(TPC // 512):
                c0 = tc * 512
                chv = self.ln_chain(self.stats, c0, wln, "ln")
                self.wait("scalar", "s_ve", C["s_ve"].v)
                for dc in range(8):
                    sl = (dc % 2) * 512
                    lvx = self.load(x_t[:, sl:sl + 512], g["xs"][dc * P:(dc + 1) * P, c0:c0 + 512],
                                    cast=True, wait=[(C["s_ve"].sem, chv if dc < 2 else vv)])
                    self.wait("vector", "s_ld", lvx)
                    self.tt(xn2[:, dc * 512:(dc + 1) * 512], x_t[:, sl:sl + 512], lnw["o1"][:, :], OP.mult)
                    vv = self.tt(xn2[:, dc * 512:(dc + 1) * 512], xn2[:, dc * 512:(dc + 1) * 512],
                                 lnw["o2"][:, :], OP.subtract, inc="s_ve")
                self.wait("tensor", "s_ve", vv)
                for oc in range(8):
                    pso = ps_o[:, (oc % 2) * 512:(oc % 2) * 512 + 512]
                    if oc < 4:
                        parts = [(wrt, 0), (nwit, 4)]
                        ocr = oc
                    else:
                        parts = [(wit, 0), (wrt, 4)]
                        ocr = oc - 4
                    k = 0
                    for wmat, dbase in parts:
                        for finc in range(4):
                            stat = wmat[:, finc * 512 + ocr * P:finc * 512 + (ocr + 1) * P]
                            mv = self.mm(pso, stat, xn2[:, (dbase + finc) * 512:(dbase + finc + 1) * 512],
                                         k == 0, k == 7, inc=("s_mm" if k == 7 else None))
                            k += 1
                    lvr = self.load(xres[:, :], g["xs"][oc * P:(oc + 1) * P, c0:c0 + 512],
                                    wait=[(C["s_st"].sem, C["s_st"].v)])
                    self.wait("vector", "s_mm", mv)
                    self.wait("vector", "s_ld", lvr)
                    self.vcopy(ev_t[:, :], pso)
                    vv2 = self.stt(ev_t[:, :], ev_t[:, :], ffnb[:, oc:oc + 1], xres[:, :],
                                   OP.add, OP.add, inc="s_ve")
                    self.store(g["xs"][oc * P:(oc + 1) * P, c0:c0 + 512], ev_t[:, :],
                               wait=[(C["s_ve"].sem, vv2)])
                    self.wait("tensor", "s_ve", vv2)
                    self.wait("vector", "s_st", C["s_st"].v)

    def p3_attn(self):
        g, nc, C = self.g, self.nc, self.C

        with ExitStack() as px:
            sbl = lambda n, s, d=F32: px.enter_context(nc.sbuf_tensor(n, list(s), d))
            wq3 = sbl("wq3", [P, 8 * DM], BF16)
            wk3 = sbl("wk3", [P, 8 * DM], BF16)
            wv3 = sbl("wv3", [P, 8 * DM], BF16)
            ropc = sbl("ropc", [P, 4 * LSEQ], BF16)
            rops = sbl("rops", [P, 4 * LSEQ], BF16)
            x_t = sbl("p3x_t", [P, 2 * 512], BF16)
            xq_t = sbl("p3xq_t", [P, 512], BF16)
            xn3 = sbl("xn3", [P, 8 * 512], BF16)
            q1 = sbl("q1", [P, 8 * 512], BF16)
            rt1 = sbl("rt1", [P, 512])
            rt2 = sbl("rt2", [P, 512])
            ro_t = sbl("ro_t", [P, 512], BF16)
            lnw = {}
            for nm in ["m", "v", "t1", "t2", "o1", "o2"]:
                lnw[nm] = sbl("p3lnw_" + nm, [P, 512])
            ps_st = px.enter_context(nc.psum_tensor("p3ps_st", [P, 512], F32))
            ps_e = px.enter_context(nc.psum_tensor("p3ps_e", [P, 512], F32))
            ps_o = px.enter_context(nc.psum_tensor("p3ps_o", [P, 2 * 512], F32))

            for dc in range(8):
                self.load(wq3[:, dc * DM:(dc + 1) * DM], g["wq_t"][dc * P:(dc + 1) * P, :], cast=True)
                self.load(wk3[:, dc * DM:(dc + 1) * DM], g["wk_t"][dc * P:(dc + 1) * P, :], cast=True)
                self.load(wv3[:, dc * DM:(dc + 1) * DM], g["wv_t"][dc * P:(dc + 1) * P, :], cast=True)
            for pc in range(4):
                self.load(ropc[:, pc * LSEQ:(pc + 1) * LSEQ], g["c_rope_cos"][pc * P:(pc + 1) * P, :], cast=True)
                lwv = self.load(rops[:, pc * LSEQ:(pc + 1) * LSEQ], g["c_rope_sin"][pc * P:(pc + 1) * P, :], cast=True)
            self.wait("tensor", lwv[0], lwv[1])
            self.wait("vector", lwv[0], lwv[1])

            self.stats_pass(x_t, xq_t, ps_st)

            wln = dict(lnw); wln["ps_m"] = ps_st; wln["ps_e"] = ps_e
            vv = C["s_ve"].v
            for tc in range(TPC // 512):
                c0 = tc * 512
                pos0 = (tc % 4) * 512
                chv = self.ln_chain(self.stats, c0, wln, "ln")
                self.wait("scalar", "s_ve", C["s_ve"].v)
                for dc in range(8):
                    sl = (dc % 2) * 512
                    lvx = self.load(x_t[:, sl:sl + 512], g["xs"][dc * P:(dc + 1) * P, c0:c0 + 512],
                                    cast=True, wait=[(C["s_ve"].sem, chv if dc < 2 else vv)])
                    self.wait("vector", lvx[0], lvx[1])
                    self.tt(xn3[:, dc * 512:(dc + 1) * 512], x_t[:, sl:sl + 512], lnw["o1"][:, :], OP.mult)
                    vv = self.tt(xn3[:, dc * 512:(dc + 1) * 512], xn3[:, dc * 512:(dc + 1) * 512],
                                 lnw["o2"][:, :], OP.subtract, inc="s_ve")
                self.wait("tensor", "s_ve", vv)
                for proj, wmat, dst in [("q", wq3, "sc_q"), ("k", wk3, "sc_k"), ("v", wv3, "sc_v")]:
                    self.wait("scalar", "s_ve", C["s_ve"].v)
                    for oc in range(8):
                        pso = ps_o[:, (oc % 2) * 512:(oc % 2) * 512 + 512]
                        for dc in range(8):
                            stat = wmat[:, dc * DM + oc * P:dc * DM + (oc + 1) * P]
                            mv = self.mm(pso, stat, xn3[:, dc * 512:(dc + 1) * 512], dc == 0, dc == 7,
                                         inc=("s_mm" if dc == 7 else None))
                        self.wait("scalar", "s_mm", mv)
                        av = self.act(q1[:, oc * 512:(oc + 1) * 512], pso, AF.Copy, inc="s_ac")
                        self.wait("tensor", "s_ac", av)
                    self.wait("vector", "s_ac", av)
                    for k in range(4):
                        cc = ropc[:, k * LSEQ + pos0:k * LSEQ + pos0 + 512]
                        ss = rops[:, k * LSEQ + pos0:k * LSEQ + pos0 + 512]
                        qe = q1[:, k * 512:(k + 1) * 512]
                        qo = q1[:, (4 + k) * 512:(5 + k) * 512]
                        self.tt(rt1[:, :], qe, cc, OP.mult)
                        self.tt(rt2[:, :], qo, ss, OP.mult)
                        vv = self.tt(ro_t[:, :], rt1[:, :], rt2[:, :], OP.add, inc="s_ve")
                        self.store(g[dst][k * P:(k + 1) * P, c0:c0 + 512], ro_t[:, :],
                                   wait=[(C["s_ve"].sem, vv)])
                        self.wait("vector", "s_st", C["s_st"].v)
                        self.tt(rt1[:, :], qo, cc, OP.mult)
                        self.tt(rt2[:, :], qe, ss, OP.mult)
                        vv = self.tt(ro_t[:, :], rt1[:, :], rt2[:, :], OP.subtract, inc="s_ve")
                        self.store(g[dst][(4 + k) * P:(5 + k) * P, c0:c0 + 512], ro_t[:, :],
                                   wait=[(C["s_ve"].sem, vv)])
                        self.wait("vector", "s_st", C["s_st"].v)
        self.barrier()

        with ExitStack() as px:
            sbl = lambda n, s, d=F32: px.enter_context(nc.sbuf_tensor(n, list(s), d))
            wqi = sbl("wqi", [P, 8 * DM], BF16)
            wki = sbl("wki", [P, 8 * DM], BF16)
            wvi = sbl("wvi", [P, 8 * DM], BF16)
            bq3 = sbl("bq3", [P, 8])
            bk3 = sbl("bk3", [P, 8])
            bvi_f = sbl("bvi_f", [1, DM])
            bvi_b = sbl("bvi_b", [1, DM], BF16)
            m_t = sbl("p3m_t", [P, 2 * 512], BF16)
            ev_t = sbl("p3ev_t", [P, 512], BF16)
            vt_t = sbl("p3vt_t", [P, DM], BF16)
            ps_a = px.enter_context(nc.psum_tensor("p3ps_a", [P, 2 * 512], F32))
            ps_v = px.enter_context(nc.psum_tensor("p3ps_v", [P, 2 * 512], F32))
            for dc in range(8):
                self.load(wqi[:, dc * DM:(dc + 1) * DM], g["wqi_t"][dc * P:(dc + 1) * P, :], cast=True)
                self.load(wki[:, dc * DM:(dc + 1) * DM], g["wki_t"][dc * P:(dc + 1) * P, :], cast=True)
                lwv = self.load(wvi[:, dc * DM:(dc + 1) * DM], g["wvi_t"][dc * P:(dc + 1) * P, :], cast=True)
            self.load(bq3[:, :], g["bqi"][:, :])
            self.load(bk3[:, :], g["bki"][:, :])
            lvb = self.load(bvi_f[:, :], g["bvi"][:, :])
            self.wait("vector", lvb[0], lvb[1])
            bcv = self.vcopy(bvi_b[:, :], bvi_f[:, :], inc="s_ve")
            self.wait("tensor", "s_ve", bcv)
            self.wait("tensor", lwv[0], lwv[1])
            self.wait("scalar", lvb[0], lvb[1])

            mall = sbl("p3mall", [P, 8 * 512], BF16)
            for src_nm, wmat, bias, dst in [("sc_q", wqi, bq3, "sc_qh"), ("sc_k", wki, bk3, "sc_kh")]:
                for tc in range(TPC // 512):
                    c0 = tc * 512
                    for dc in range(8):
                        lvx = self.load(mall[:, dc * 512:(dc + 1) * 512],
                                        g[src_nm][dc * P:(dc + 1) * P, c0:c0 + 512],
                                        wait=([(C["s_mm"].sem, C["s_mm"].v)] if dc == 0 else ()))
                    self.wait("tensor", lvx[0], lvx[1])
                    for oc in range(8):
                        pso = ps_a[:, (oc % 2) * 512:(oc % 2) * 512 + 512]
                        for dc in range(8):
                            mv = self.mm(pso, wmat[:, dc * DM + oc * P:dc * DM + (oc + 1) * P],
                                         mall[:, dc * 512:(dc + 1) * 512], dc == 0, dc == 7,
                                         inc=("s_mm" if dc == 7 else None))
                        self.wait("vector", "s_mm", mv)
                        av = self.tscal(ev_t[:, :], pso, bias[:, oc:oc + 1],
                                        self.consts["ones_col_f"][:, 0:1], OP.add, OP.mult, inc="s_ve")
                        self.store(g[dst][oc * P:(oc + 1) * P, c0:c0 + 512], ev_t[:, :],
                                   wait=[(C["s_ve"].sem, av)])
                        self.wait("vector", "s_st", C["s_st"].v)
                        self.wait("tensor", "s_ve", av)

            for tt in range(TPC // P):
                c0 = tt * P
                for dc in range(8):
                    lvx = self.load(m_t[:, (dc % 2) * 512:(dc % 2) * 512 + P],
                                    g["sc_v"][dc * P:(dc + 1) * P, c0:c0 + P],
                                    wait=[(C["s_mm"].sem, C["s_mm"].v)])
                    self.wait("tensor", lvx[0], lvx[1])
                    for hh in range(2):
                        self.mm(ps_v[:, hh * 512:(hh + 1) * 512],
                                m_t[:, (dc % 2) * 512:(dc % 2) * 512 + P],
                                wvi[:, dc * DM + hh * 512:dc * DM + (hh + 1) * 512],
                                dc == 0, False,
                                inc=("s_mm" if hh == 1 else None))
                for hh in range(2):
                    mv = self.mm(ps_v[:, hh * 512:(hh + 1) * 512], self.consts["ones_row_b"][:, :],
                                 bvi_b[:, hh * 512:(hh + 1) * 512], False, True,
                                 inc="s_mm")
                self.wait("vector", "s_mm", mv)
                vv = self.vcopy(vt_t[:, :], ps_v[:, :], inc="s_ve")
                self.store(g["sc_vh_tm"][c0:c0 + P, :], vt_t[:, :], wait=[(C["s_ve"].sem, vv)])
                self.wait("vector", "s_st", C["s_st"].v)
                self.wait("tensor", "s_ve", vv)
        self.barrier()

        with ExitStack() as px:
            sbl = lambda n, s, d=F32: px.enter_context(nc.sbuf_tensor(n, list(s), d))
            qh_t = sbl("qh_t", [64, LSEQ], BF16)
            kh_t = sbl("kh_t", [64, LSEQ], BF16)
            vaug = sbl("vaug", [P, 16 * 65], BF16)
            pt_t = sbl("pt_t", [P, 16 * 512], BF16)
            osb = sbl("osb", [64, 512], BF16)
            ofin = sbl("ofin", [64, 512], BF16)
            lrec = sbl("lrec", [P, 512])
            ps_s = px.enter_context(nc.psum_tensor("p3ps_s", [P, 2 * 512], F32))
            ps_av = px.enter_context(nc.psum_tensor("p3ps_av", [P, 512], F32))
            ps_l = px.enter_context(nc.psum_tensor("p3ps_l", [1, 512], F32))
            ps_lr = px.enter_context(nc.psum_tensor("p3ps_lr", [P, 512], F32))
            for kc in range(16):
                self.vmemset(vaug[:, kc * 65 + 64:kc * 65 + 65], 1.0)
            ivv = self.emit("vector", lambda e: e.engine_nop(), inc="s_ve")
            self.wait("sync", "s_ve", ivv)

            for seq in range(NSEQ):
                for h in range(NHEAD):
                    sb0 = seq * LSEQ
                    lq = self.load(qh_t[:, :], g["sc_qh"][h * HD:(h + 1) * HD, sb0:sb0 + LSEQ],
                                   wait=[(C["s_mm"].sem, C["s_mm"].v)])
                    lk = self.load(kh_t[:, :], g["sc_kh"][h * HD:(h + 1) * HD, sb0:sb0 + LSEQ])
                    vsrc = g["sc_vh_tm"][sb0:sb0 + LSEQ, h * HD:(h + 1) * HD]
                    vdst = vaug[:, 0:16 * 65].rearrange("p (k c) -> p k c", c=65)[:, :, 0:HD]
                    lv3 = self.load(vdst, vsrc.rearrange("(k p) c -> p k c", p=P))
                    self.wait("tensor", lq[0], lq[1])
                    self.wait("tensor", lv3[0], lv3[1])
                    for qt in range(4):
                        q0 = qt * 512
                        nkc = 4 * qt + 4
                        avl = C["s_ac"].v
                        vvl = C["s_ve"].v
                        for kc in range(nkc):
                            pss = ps_s[:, (kc % 2) * 512:(kc % 2) * 512 + 512]
                            mv = self.mm(pss, kh_t[:, kc * P:(kc + 1) * P], qh_t[:, q0:q0 + 512],
                                         True, True, inc="s_mm")
                            self.wait("scalar", "s_mm", mv)
                            avl = self.act(pt_t[:, kc * 512:(kc + 1) * 512], pss, AF.Exp,
                                           scale=0.125, inc="s_ac")
                            self.wait("tensor", "s_ac", avl)
                            if kc >= 4 * qt:
                                self.wait("vector", "s_ac", avl)
                                vvl = self.tt(pt_t[:, kc * 512:(kc + 1) * 512],
                                              pt_t[:, kc * 512:(kc + 1) * 512],
                                              self.consts["diagmask"][:, (kc - 4 * qt) * 512:(kc - 4 * qt + 1) * 512],
                                              OP.mult, inc="s_ve")
                        self.wait("tensor", "s_ve", vvl)
                        self.wait("tensor", "s_ac", avl)
                        for kc in range(nkc):
                            self.mm(ps_av[0:64, :], vaug[:, kc * 65:kc * 65 + 64],
                                    pt_t[:, kc * 512:(kc + 1) * 512], kc == 0, kc == nkc - 1)
                            mv = self.mm(ps_l[0:1, :], self.consts["stat_sx"][:, 0:1],
                                         pt_t[:, kc * 512:(kc + 1) * 512], kc == 0, kc == nkc - 1,
                                         inc=("s_mm" if kc == nkc - 1 else None))
                        self.wait("vector", "s_mm", mv)
                        self.recip(lrec[0:1, :], ps_l[0:1, :])
                        vv = self.vcopy(osb[:, :], ps_av[0:64, :], inc="s_ve")
                        self.wait("tensor", "s_ve", vv)
                        mv = self.mm(ps_lr[:, :], self.consts["ones_row_f"][:, :], lrec[0:1, :],
                                     True, True, inc="s_mm")
                        self.wait("vector", "s_mm", mv)
                        vv = self.tt(ofin[:, :], osb[:, :], ps_lr[0:64, :], OP.mult, inc="s_ve")
                        self.store(g["sc_o"][h * HD:(h + 1) * HD, sb0 + q0:sb0 + q0 + 512], ofin[:, :],
                                   wait=[(C["s_ve"].sem, vv)])
                        self.wait("vector", "s_st", C["s_st"].v)
                        self.wait("tensor", "s_ve", vv)
        self.barrier()

        with ExitStack() as px:
            sbl = lambda n, s, d=F32: px.enter_context(nc.sbuf_tensor(n, list(s), d))
            wo3 = sbl("wo3", [P, 8 * DM], BF16)
            ob3 = sbl("ob3", [P, 8])
            mall = sbl("p3dmall", [P, 8 * 512], BF16)
            xres = sbl("p3xres", [P, 512])
            for dc in range(8):
                lwv = self.load(wo3[:, dc * DM:(dc + 1) * DM], g["out_w_t"][dc * P:(dc + 1) * P, :], cast=True)
            lvb = self.load(ob3[:, :], g["out_b"][:, :])
            self.wait("tensor", lwv[0], lwv[1])
            self.wait("vector", lvb[0], lvb[1])
            ps_a = px.enter_context(nc.psum_tensor("p3dps_a", [P, 2 * 512], F32))
            for tc in range(TPC // 512):
                c0 = tc * 512
                for dc in range(8):
                    lvx = self.load(mall[:, dc * 512:(dc + 1) * 512],
                                    g["sc_o"][dc * P:(dc + 1) * P, c0:c0 + 512],
                                    wait=([(C["s_mm"].sem, C["s_mm"].v)] if dc == 0 else ()))
                self.wait("tensor", lvx[0], lvx[1])
                for oc in range(8):
                    pso = ps_a[:, (oc % 2) * 512:(oc % 2) * 512 + 512]
                    for dc in range(8):
                        mv = self.mm(pso, wo3[:, dc * DM + oc * P:dc * DM + (oc + 1) * P],
                                     mall[:, dc * 512:(dc + 1) * 512], dc == 0, dc == 7,
                                     inc=("s_mm" if dc == 7 else None))
                    lvr = self.load(xres[:, :], g["xs"][oc * P:(oc + 1) * P, c0:c0 + 512],
                                    wait=[(C["s_st"].sem, C["s_st"].v)])
                    self.wait("vector", "s_mm", mv)
                    self.wait("vector", lvr[0], lvr[1])
                    vv2 = self.stt(xres[:, :], pso, ob3[:, oc:oc + 1], xres[:, :],
                                   OP.add, OP.add, inc="s_ve")
                    self.store(g["xs"][oc * P:(oc + 1) * P, c0:c0 + 512], xres[:, :],
                               wait=[(C["s_ve"].sem, vv2)])
                    self.wait("tensor", "s_ve", vv2)


    def p4_moe(self):
        g, nc, C = self.g, self.nc, self.C
        NT = TPC // P

        oh_all = self.sb("oh_all", [P, NT * 16])
        excl_sb = self.sb("excl_sb", [P, NT * 8])
        route_sb = self.sb("route_sb", [P, NT * 8])
        idx_all = self.sb("idx_all", [P, NT * 2], I32)
        tot_sb = self.sb("tot_sb", [1, NT * 8])
        off_sb = self.sb("off_sb", [1, (NT + 1) * 8])
        base_sb = self.sb("base_sb", [1, 8])
        brep_sb = self.sb("brep_sb", [P, 8])
        agsb = self.sb("agsb", [8, 8])

        with ExitStack() as px:
            sbl = lambda n, s, d=F32: px.enter_context(nc.sbuf_tensor(n, list(s), d))
            wg_sb = sbl("wg_sb", [P, 8 * 8], BF16)
            x_t32 = sbl("p4x32", [P, 2 * 512])
            xq_t = sbl("p4xq", [P, 512], BF16)
            xn4 = sbl("p4xn4", [P, 8 * 512], BF16)
            x3tm = sbl("x3tm", [P, 4 * DM])
            xn4tm = sbl("xn4tm", [P, 4 * DM], BF16)
            oh12b = sbl("oh12b", [P, 8], BF16)
            w8a = sbl("p4w8a", [P, 8])
            w8b = sbl("p4w8b", [P, 8])
            w1a = sbl("p4w1a", [P, 1])
            w1b = sbl("p4w1b", [P, 1])
            w1c = sbl("p4w1c", [P, 1])
            lnw = {}
            for nm in ["m", "v", "t1", "t2", "o1", "o2"]:
                lnw[nm] = sbl("p4lnw_" + nm, [P, 512])
            ps_st = px.enter_context(nc.psum_tensor("p4ps_st", [P, 512], F32))
            ps_e = px.enter_context(nc.psum_tensor("p4ps_e", [P, 512], F32))
            ps_tp = px.enter_context(nc.psum_tensor("p4ps_tp", [P, 512], F32))
            ps_tpb = px.enter_context(nc.psum_tensor("p4ps_tpb", [P, P], BF16))
            ps_sm = px.enter_context(nc.psum_tensor("p4ps_sm", [P, 512], F32))

            for dc in range(8):
                lwv = self.load(wg_sb[:, dc * 8:(dc + 1) * 8], g["Wg"][dc * P:(dc + 1) * P, :], cast=True)
            self.wait("tensor", lwv[0], lwv[1])
            self.stats_pass(x_t32[:, :].bitcast(BF16)[:, 0:1024], xq_t, ps_st)

            wln = dict(lnw); wln["ps_m"] = ps_st; wln["ps_e"] = ps_e
            vv = C["s_ve"].v
            for tc in range(TPC // 512):
                c0 = tc * 512
                chv = self.ln_chain(self.stats, c0, wln, "ln")
                self.wait("scalar", "s_ve", C["s_ve"].v)
                for dc in range(8):
                    sl = (dc % 2) * 512
                    lvx = self.load(x_t32[:, sl:sl + 512], g["xs"][dc * P:(dc + 1) * P, c0:c0 + 512],
                                    wait=[(C["s_ve"].sem, chv if dc < 2 else vv),
                                          (C["s_mm"].sem, C["s_mm"].v)])
                    self.wait("vector", lvx[0], lvx[1])
                    self.tt(xn4[:, dc * 512:(dc + 1) * 512], x_t32[:, sl:sl + 512], lnw["o1"][:, :], OP.mult)
                    vv = self.tt(xn4[:, dc * 512:(dc + 1) * 512], xn4[:, dc * 512:(dc + 1) * 512],
                                 lnw["o2"][:, :], OP.subtract, inc="s_ve")
                    self.wait("tensor", "s_ve", vv)
                    for sub in range(4):
                        mvt = self.transpose_mm(ps_tp[:, 0:P], x_t32[:, sl + sub * P:sl + (sub + 1) * P],
                                                False, inc="s_mm")
                        self.wait("vector", "s_mm", mvt)
                        vv = self.vcopy(x3tm[:, sub * DM + dc * P:sub * DM + (dc + 1) * P],
                                        ps_tp[:, 0:P], inc="s_ve")
                        self.wait("tensor", "s_ve", vv)
                        mvt = self.transpose_mm(ps_tpb[:, 0:P],
                                                xn4[:, dc * 512 + sub * P:dc * 512 + (sub + 1) * P],
                                                True, inc="s_mm")
                        self.wait("vector", "s_mm", mvt)
                        vv = self.vcopy(xn4tm[:, sub * DM + dc * P:sub * DM + (dc + 1) * P],
                                        ps_tpb[:, 0:P], inc="s_ve")
                        self.wait("tensor", "s_ve", vv)
                for sub in range(4):
                    tile = tc * 4 + sub
                    self.store(g["sc_x3_tm"][tile * P:(tile + 1) * P, :], x3tm[:, sub * DM:(sub + 1) * DM],
                               wait=[(C["s_ve"].sem, vv)])
                    self.store(g["sc_xn4_tm"][tile * P:(tile + 1) * P, :], xn4tm[:, sub * DM:(sub + 1) * DM])
                self.wait("vector", "s_st", C["s_st"].v)
                for sub in range(4):
                    tile = tc * 4 + sub
                    for dc in range(8):
                        mv = self.mm(ps_sm[:, 0:8], xn4[:, dc * 512 + sub * P:dc * 512 + (sub + 1) * P],
                                     wg_sb[:, dc * 8:(dc + 1) * 8], dc == 0, dc == 7,
                                     inc=("s_mm" if dc == 7 else None))
                    self.wait("vector", "s_mm", mv)
                    lgt = w8a
                    self.vcopy(lgt[:, :], ps_sm[:, 0:8])
                    self.tscal(w8b[:, :], self.consts["iota8"][:, :], 1e-6, None, OP.mult)
                    self.tt(w8b[:, :], lgt[:, :], w8b[:, :], OP.subtract)
                    self.vreduce(w1a[:, 0:1], w8b[:, :], OP.max)
                    self.tscal(oh_all[:, tile * 16:tile * 16 + 8], w8b[:, :], w1a[:, 0:1],
                               self.consts["ones_col_f"][:, 0:1], OP.is_equal, OP.mult)
                    self.tscal(lnw["t1"][:, 0:8], oh_all[:, tile * 16:tile * 16 + 8], 1e9, None, OP.mult)
                    self.tt(lnw["t1"][:, 0:8], w8b[:, :], lnw["t1"][:, 0:8], OP.subtract)
                    self.vreduce(w1a[:, 0:1], lnw["t1"][:, 0:8], OP.max)
                    self.tscal(oh_all[:, tile * 16 + 8:tile * 16 + 16], lnw["t1"][:, 0:8], w1a[:, 0:1],
                               self.consts["ones_col_f"][:, 0:1], OP.is_equal, OP.mult)
                    self.tt(w8b[:, :], oh_all[:, tile * 16:tile * 16 + 8], lgt[:, :], OP.mult)
                    self.vreduce(w1a[:, 0:1], w8b[:, :])
                    self.tt(w8b[:, :], oh_all[:, tile * 16 + 8:tile * 16 + 16], lgt[:, :], OP.mult)
                    self.vreduce(w1b[:, 0:1], w8b[:, :])
                    vv = self.tt(w1b[:, 0:1], w1b[:, 0:1], w1a[:, 0:1], OP.subtract, inc="s_ve")
                    self.wait("scalar", "s_ve", vv)
                    av = self.act(w1c[:, 0:1], w1b[:, 0:1], AF.Exp, inc="s_ac")
                    self.wait("vector", "s_ac", av)
                    self.tscal(w1a[:, 0:1], w1c[:, 0:1], 1.0, None, OP.add)
                    self.recip(w1a[:, 0:1], w1a[:, 0:1])
                    self.vcopy(route_sb[:, tile * 8:tile * 8 + 1], w1a[:, 0:1])
                    self.tt(route_sb[:, tile * 8 + 1:tile * 8 + 2], w1c[:, 0:1], w1a[:, 0:1], OP.mult)
                    self.tt(w8a[:, :], oh_all[:, tile * 16:tile * 16 + 8],
                            oh_all[:, tile * 16 + 8:tile * 16 + 16], OP.add)
                    vv = self.vcopy(oh12b[:, :], w8a[:, :], inc="s_ve")
                    self.wait("tensor", "s_ve", vv)
                    self.mm(ps_sm[:, 16:24], self.consts["lt_b"][:, :], oh12b[:, :], True, True)
                    mv = self.mm(ps_sm[0:1, 32:40], self.consts["stat_sx"][:, 0:1], oh12b[:, :],
                                 True, True, inc="s_mm")
                    self.wait("vector", "s_mm", mv)
                    self.vcopy(excl_sb[:, tile * 8:(tile + 1) * 8], ps_sm[:, 16:24])
                    vv = self.vcopy(tot_sb[0:1, tile * 8:(tile + 1) * 8], ps_sm[0:1, 32:40], inc="s_ve")
                    self.wait("tensor", "s_ve", vv)

            self.vmemset(off_sb[0:1, 0:8], 0.0)
            for k in range(NT):
                vv = self.tt(off_sb[0:1, (k + 1) * 8:(k + 2) * 8], off_sb[0:1, k * 8:(k + 1) * 8],
                             tot_sb[0:1, k * 8:(k + 1) * 8], OP.add, inc="s_ve")
            stv = self.store(g["agin"][:, :], off_sb[0:1, NT * 8:(NT + 1) * 8],
                             wait=[(C["s_ve"].sem, vv)])
            def _coll(e, i=g["agin"][:, :], o=g["agout"][:, :], sem=C["s_gp"].sem, w=(C["s_st"].sem, stv)):
                e.wait_ge(w[0], w[1])
                e.collective_compute("AllGather", OP.bypass, replica_groups=[list(range(NCORES))],
                                     ins=[i], outs=[o]).then_inc(sem, 1)
            gv = self.C["s_gp"].inc(1)
            self.plan.em("gpsimd", _coll)
            lva = self.load(agsb[:, :], g["agout"][:, :], wait=[(C["s_gp"].sem, gv)])
            self.wait("tensor", lva[0], lva[1])
            mv = self.mm(ps_sm[0:1, 48:56], self.consts["coremask"][:, 0:1], agsb[:, :],
                         True, True, inc="s_mm")
            self.wait("vector", "s_mm", mv)
            vv = self.vcopy(base_sb[0:1, :], ps_sm[0:1, 48:56], inc="s_ve")
            self.wait("tensor", "s_ve", vv)
            mv = self.mm(ps_sm[:, 64:72], self.consts["ones_row_f"][:, :], base_sb[0:1, :],
                         True, True, inc="s_mm")
            self.wait("vector", "s_mm", mv)
            vv = self.vcopy(brep_sb[:, :], ps_sm[:, 64:72], inc="s_ve")
            self.wait("tensor", "s_ve", vv)

            for tile in range(NT):
                mv = self.mm(ps_sm[:, 80:88], self.consts["ones_row_f"][:, :],
                             off_sb[0:1, tile * 8:(tile + 1) * 8], True, True, inc="s_mm")
                self.wait("vector", "s_mm", mv)
                for k in range(2):
                    ohk = oh_all[:, tile * 16 + 8 * k:tile * 16 + 8 * (k + 1)]
                    self.tt(w8a[:, :], ohk, excl_sb[:, tile * 8:(tile + 1) * 8], OP.mult)
                    self.vreduce(w1a[:, 0:1], w8a[:, :])
                    self.tt(w8a[:, :], ohk, ps_sm[:, 80:88], OP.mult)
                    self.vreduce(w1b[:, 0:1], w8a[:, :])
                    self.tt(w1a[:, 0:1], w1a[:, 0:1], w1b[:, 0:1], OP.add)
                    self.tt(w8a[:, :], ohk, brep_sb[:, :], OP.mult)
                    self.vreduce(w1b[:, 0:1], w8a[:, :])
                    self.tt(w1b[:, 0:1], w1a[:, 0:1], w1b[:, 0:1], OP.add)
                    self.tscal(w1b[:, 0:1], w1b[:, 0:1], float(CAPG), self.consts["ones_col_f"][:, 0:1],
                               OP.is_lt, OP.mult)
                    self.tscal(w1c[:, 0:1], w1a[:, 0:1], float(CAPL), self.consts["ones_col_f"][:, 0:1],
                               OP.is_lt, OP.mult)
                    self.tt(w1b[:, 0:1], w1b[:, 0:1], w1c[:, 0:1], OP.mult)
                    gcol = route_sb[:, tile * 8 + k:tile * 8 + k + 1]
                    self.tt(gcol, gcol, w1b[:, 0:1], OP.mult)
                    self.tt(w8a[:, :], ohk, self.consts["iota8"][:, :], OP.mult)
                    self.vreduce(w1b[:, 0:1], w8a[:, :])
                    self.tscal(w1b[:, 0:1], w1b[:, 0:1], float(CAPL), None, OP.mult)
                    self.tt(w1b[:, 0:1], w1b[:, 0:1], w1a[:, 0:1], OP.add)
                    self.tscal(w1b[:, 0:1], w1b[:, 0:1], -float(TRASH), None, OP.add)
                    self.tt(w1b[:, 0:1], w1b[:, 0:1], w1c[:, 0:1], OP.mult)
                    self.tscal(w1b[:, 0:1], w1b[:, 0:1], float(TRASH), None, OP.add)
                    vv = self.vcopy(idx_all[:, tile * 2 + k:tile * 2 + k + 1], w1b[:, 0:1], inc="s_ve")
                self.wait("tensor", "s_ve", vv)

            scv = C["s_ve"].v
            sgl = C["s_lg"].v
            for tile in range(NT):
                sl = (tile % 4) * DM
                ltl = self.load(xn4tm[:, sl:sl + DM], g["sc_xn4_tm"][tile * P:(tile + 1) * P, :],
                                wait=[(C["s_lg"].sem, sgl)])
                for k in range(2):
                    tgt = self.C["s_lg"].inc(16)
                    def _sc(e, t=tile, kk=k, s0=sl, w=(C["s_ve"].sem, scv),
                            wl=(C["s_ld"].sem, ltl[1]), sem=C["s_lg"].sem, first=(k == 0),
                            p=tgt - 16):
                        e.wait_ge(sem, p)
                        if first:
                            e.wait_ge(w[0], w[1])
                            e.wait_ge(wl[0], wl[1])
                        e.indirect_dma_start(
                            out=g["sc_ein"][:, :],
                            out_offset=IOOA(idx_all[:, t * 2 + kk:t * 2 + kk + 1], 0),
                            in_=xn4tm[:, s0:s0 + DM], in_offset=None,
                        ).then_inc(sem, 16)
                    self.plan.em("gpsimd", _sc)
                if tile % 4 == 3:
                    sgl = self.C["s_lg"].v
            scdone = ("s_lg", self.C["s_lg"].v)
        self.barrier()

        with ExitStack() as px:
            sbl = lambda n, s, d=F32: px.enter_context(nc.sbuf_tensor(n, list(s), d))
            w1sb = sbl("w1sb", [P, 8 * 4096], BF16)
            xe = sbl("xe", [P, 8 * CAPL], BF16)
            ghb = sbl("ghb", [P, 2 * CAPL], BF16)
            ght = sbl("ght", [P, 2 * P], BF16)
            b1sb = sbl("b1sb", [P, 32])
            evb = sbl("evb", [P, DM], BF16)
            b2row = sbl("b2row", [1, DM], BF16)
            b2f = sbl("b2f", [1, DM])
            ps_h = px.enter_context(nc.psum_tensor("p4ps_h", [P, 2 * 512], F32))
            ps_y = px.enter_context(nc.psum_tensor("p4ps_y", [P, 2 * 512], F32))

            for e in range(NEXP):
                for dc in range(8):
                    lw = self.load(w1sb[:, dc * 4096:(dc + 1) * 4096],
                                   g["W1"][e, dc * P:(dc + 1) * P, :], cast=True,
                                   wait=([(C["s_lg"].sem, scdone[1]), (C["s_mm"].sem, C["s_mm"].v)] if dc == 0 else ()))
                lb = self.load(b1sb[:, :], g["b1"][e, :, :])
                lb2 = self.load(b2f[:, :], g["b2"][e:e + 1, :])
                for dc in range(8):
                    lxe = ("s_ld", self.C["s_ld"].inc(16))
                    def _tp(e2, o=xe[:, dc * CAPL:(dc + 1) * CAPL],
                            i=g["sc_ein"][e * CAPL:(e + 1) * CAPL, dc * P:(dc + 1) * P],
                            sem=C["s_ld"].sem, first=(dc == 0),
                            w1=(C["s_lg"].sem, scdone[1]), w2=(C["s_mm"].sem, C["s_mm"].v),
                            p=lxe[1] - 16):
                        e2.wait_ge(sem, p)
                        if first:
                            e2.wait_ge(w1[0], w1[1])
                            e2.wait_ge(w2[0], w2[1])
                        e2.dma_start(out=o, in_=i, transpose=True).then_inc(sem, 16)
                    self.plan.em("sync", _tp)
                self.wait("vector", lb2[0], lb2[1])
                bcv = self.vcopy(b2row[:, :], b2f[:, :], inc="s_ve")
                self.wait("tensor", "s_ve", bcv)
                self.wait("tensor", lw[0], lw[1])
                self.wait("tensor", lxe[0], lxe[1])
                self.wait("scalar", lb[0], lb[1])
                for hc in range(32):
                    for sl in range(4):
                        psh = ps_h[:, (sl % 2) * 512:(sl % 2) * 512 + 448]
                        for dc in range(8):
                            mv = self.mm(psh, w1sb[:, dc * 4096 + hc * P:dc * 4096 + (hc + 1) * P],
                                         xe[:, dc * CAPL + sl * 448:dc * CAPL + (sl + 1) * 448],
                                         dc == 0, dc == 7, inc=("s_mm" if dc == 7 else None))
                        self.wait("scalar", "s_mm", mv)
                        av = self.act(ghb[:, sl * 448:(sl + 1) * 448], psh, AF.Gelu_apprx_tanh,
                                      bias=b1sb[:, hc:hc + 1], inc="s_ac")
                        self.wait("tensor", "s_ac", av)
                    avv = C["s_ac"].v
                    self.store(g["sc_gh"][hc * P:(hc + 1) * P, :], ghb[:, 0:CAPL],
                               wait=[(C["s_ac"].sem, avv)])
                    self.wait("scalar", "s_st", C["s_st"].v)
                for hc2 in range(8):
                    lw = self.load(w1sb[:, hc2 * 4096:(hc2 + 1) * 4096].rearrange("p (a c) -> p a c", a=4),
                                   g["W2"][e, hc2 * 512:(hc2 + 1) * 512, :].rearrange("(a b) c -> b a c", b=P),
                                   cast=True,
                                   wait=([(C["s_mm"].sem, C["s_mm"].v)] if hc2 == 0 else ()))
                self.wait("tensor", lw[0], lw[1])
                for sg in range(14):
                    for hc in range(32):
                        lgh = self.load(ght[:, (hc % 2) * P:(hc % 2) * P + P],
                                        g["sc_gh"][hc * P:(hc + 1) * P, sg * P:(sg + 1) * P],
                                        wait=[(C["s_mm"].sem, C["s_mm"].v)])
                        self.wait("tensor", lgh[0], lgh[1])
                        for half in range(2):
                            w2slice = w1sb[:, (hc // 4) * 4096 + (hc % 4) * 1024 + half * 512:
                                           (hc // 4) * 4096 + (hc % 4) * 1024 + (half + 1) * 512]
                            mv = self.mm(ps_y[:, half * 512:(half + 1) * 512],
                                         ght[:, (hc % 2) * P:(hc % 2) * P + P], w2slice,
                                         hc == 0, False,
                                         inc=("s_mm" if half == 1 else None))
                    for half in range(2):
                        mv = self.mm(ps_y[:, half * 512:(half + 1) * 512], self.consts["ones_row_b"][:, :],
                                     b2row[0:1, half * 512:(half + 1) * 512], False, True, inc="s_mm")
                    self.wait("vector", "s_mm", mv)
                    vv = self.vcopy(evb[:, :], ps_y[:, 0:1024], inc="s_ve")
                    self.wait("tensor", "s_ve", vv)
                    self.store(g["sc_eout"][e * CAPL + sg * P:e * CAPL + (sg + 1) * P, :], evb[:, :],
                               wait=[(C["s_ve"].sem, vv)])
                    self.wait("vector", "s_st", C["s_st"].v)
        self.barrier()

        with ExitStack() as px:
            sbl = lambda n, s, d=F32: px.enter_context(nc.sbuf_tensor(n, list(s), d))
            y0 = sbl("p4y0", [P, DM], BF16)
            y1 = sbl("p4y1", [P, DM], BF16)
            x3t = sbl("p4x3t", [P, DM])
            acc = sbl("p4acc", [P, DM])
            zrow = sbl("p4zrow", [1, DM], BF16)
            self.vmemset(zrow[:, :], 0.0)
            zv = self.emit("vector", lambda e: e.engine_nop(), inc="s_ve")
            self.store(g["sc_eout"][TRASH:TRASH + 1, :], zrow[:, :], wait=[(C["s_ve"].sem, zv)])
            trv = ("s_st", C["s_st"].v)
            for tile in range(NT):
                vvp = C["s_ve"].v
                lg0 = ("s_lg", self.C["s_lg"].inc(16))
                def _g0(e, t=tile, w=(C["s_ve"].sem, vvp), w2=(C["s_st"].sem, trv[1]),
                        sem=C["s_lg"].sem, p=lg0[1] - 16):
                    e.wait_ge(sem, p)
                    e.wait_ge(w[0], w[1])
                    e.wait_ge(w2[0], w2[1])
                    e.indirect_dma_start(out=y0[:, :], out_offset=None, in_=g["sc_eout"][:, :],
                                         in_offset=IOOA(idx_all[:, t * 2:t * 2 + 1], 0)).then_inc(sem, 16)
                self.plan.em("gpsimd", _g0)
                lg1 = ("s_lg", self.C["s_lg"].inc(16))
                def _g1(e, t=tile, sem=C["s_lg"].sem, p=lg1[1] - 16):
                    e.wait_ge(sem, p)
                    e.indirect_dma_start(out=y1[:, :], out_offset=None, in_=g["sc_eout"][:, :],
                                         in_offset=IOOA(idx_all[:, t * 2 + 1:t * 2 + 2], 0)).then_inc(sem, 16)
                self.plan.em("gpsimd", _g1)
                lx3 = self.load(x3t[:, :], g["sc_x3_tm"][tile * P:(tile + 1) * P, :],
                                wait=[(C["s_ve"].sem, vvp)])
                self.wait("vector", lg1[0], lg1[1])
                self.wait("vector", lx3[0], lx3[1])
                self.stt(acc[:, :], y0[:, :], route_sb[:, tile * 8:tile * 8 + 1], x3t[:, :],
                         OP.mult, OP.add)
                vv = self.stt(acc[:, :], y1[:, :], route_sb[:, tile * 8 + 1:tile * 8 + 2], acc[:, :],
                              OP.mult, OP.add, inc="s_ve")
                seq = (tile * P) // LSEQ
                r0 = (tile * P) % LSEQ
                self.store(g["out"][seq, r0:r0 + P, :], acc[:, :], wait=[(C["s_ve"].sem, vv)])


def kernel(**inputs):
    in_maps = _host_prepare(inputs)
    nc = build_nc()
    res = run_bass_kernel_spmd(nc, in_maps, core_ids=list(range(NCORES)))
    out = np.concatenate([np.asarray(r["out"]) for r in res.results], axis=0)
    return out.astype(np.float32)


# version 05
# speedup vs baseline: 1.0962x; 1.0962x over previous
import numpy as np
from contextlib import ExitStack

import concourse.bass as bass
import concourse.mybir as mybir
from concourse.bass_utils import run_bass_kernel_spmd

F32 = mybir.dt.float32
BF16 = mybir.dt.bfloat16
I32 = mybir.dt.int32
AF = mybir.ActivationFunctionType
OP = mybir.AluOpType
IOOA = bass.IndirectOffsetOnAxis

P = 128
DM = 1024
DI = 2048
NST = 16
DTR = 64
LSEQ = 2048
NSEQ = 2
TPC = LSEQ * NSEQ
NCORES = 8
NHEAD = 16
HD = 64
NEXP = 8
CAPG = 8602
CAPL = 1792
TRASH = NEXP * CAPL
EPS32 = float(np.finfo(np.float32).eps)


def _expand_kron(G1, G2):
    As, Bs = [], []
    for c in range(4):
        a = G1[0, :, c, :]
        b = G2[c, :, 0, :]
        Aexp = np.repeat(np.repeat(a.T, 16, axis=0), 16, axis=1)
        Bexp = np.tile(np.tile(b.T, (32, 1)), (1, 32))
        As.append(np.ascontiguousarray(Aexp, dtype=np.float32))
        Bs.append(np.ascontiguousarray(Bexp, dtype=np.float32))
    return As, Bs


def _host_prepare(inp):
    shared = {}
    f32 = np.float32

    shared["c_ident_b"] = np.eye(P, dtype=f32)
    shared["c_ident_f"] = np.eye(P, dtype=f32)
    shared["c_ones_row"] = np.ones((1, P), dtype=f32)
    shared["c_ones_col"] = np.ones((P, 1), dtype=f32)
    shared["c_lt_strict"] = np.triu(np.ones((P, P), dtype=f32), 1)
    st_sx = np.zeros((P, 2), dtype=f32); st_sx[:, 0] = 1.0
    st_sq = np.zeros((P, 2), dtype=f32); st_sq[:, 1] = 1.0
    shared["c_stat_sx"] = st_sx
    shared["c_stat_sq"] = st_sq
    masks = np.zeros((4, P, 512), dtype=f32)
    for i in range(4):
        kk = 128 * i + np.arange(P)[:, None]
        qq = np.arange(512)[None, :]
        masks[i] = (kk <= qq).astype(f32)
    shared["c_diagmask"] = masks
    i = np.arange(DM // 2, dtype=np.float64)
    theta = 10000.0 ** (-2.0 * (i - 1.0) / DM)
    ang = np.arange(LSEQ, dtype=np.float64)[:, None] * theta[None, :]
    shared["c_rope_cos"] = np.ascontiguousarray(np.cos(ang).T, dtype=f32)
    shared["c_rope_sin"] = np.ascontiguousarray(np.sin(ang).T, dtype=f32)
    shared["c_iota8"] = np.tile(np.arange(8, dtype=f32), (P, 1))
    rs32 = np.zeros((32, 32 * P), dtype=f32)
    for r in range(32):
        rs32[r, r * P:(r + 1) * P] = 1.0
    shared["c_rowsel32"] = rs32
    rs16 = np.zeros((16, 16 * P), dtype=f32)
    for r in range(16):
        rs16[r, r * P:(r + 1) * P] = 1.0
    shared["c_rowsel16"] = rs16
    rs2 = np.zeros((2, 2 * P), dtype=f32)
    rs2[0, 0:P] = 1.0
    rs2[1, P:2 * P] = 1.0
    shared["c_rowsel2"] = rs2
    shared["c_zero_row"] = np.zeros((1, DM), dtype=f32)

    shared["w_in"] = inp["L0_attn_w_in"]
    shared["norm_w"] = inp["L0_attn_norm_w"].reshape(8, P).T.copy()
    convw = inp["L0_attn_conv_w"][:, 0, :]
    diag = np.zeros((4, 16, P, P), dtype=f32)
    for k in range(4):
        for dc in range(16):
            np.fill_diagonal(diag[k, dc], convw[dc * P:(dc + 1) * P, k])
    shared["conv_diag"] = diag
    shared["conv_b"] = inp["L0_attn_conv_b"].reshape(16, P).T.copy()
    shared["w_x"] = inp["L0_attn_w_x"]
    shared["w_dt"] = inp["L0_attn_w_dt"]
    shared["b_dt"] = inp["L0_attn_b_dt"].reshape(16, P).T.copy()
    al = inp["L0_attn_A_log"].reshape(16, P, NST)
    shared["A_log"] = np.ascontiguousarray(al.transpose(1, 0, 2).reshape(P, 256))
    shared["Dp"] = inp["L0_attn_D"].reshape(16, P).T.copy()
    shared["w_out"] = inp["L0_attn_w_out"]

    AsR, BsR = _expand_kron(inp["L0_ffn_G1_R"], inp["L0_ffn_G2_R"])
    AsI, BsI = _expand_kron(inp["L0_ffn_G1_I"], inp["L0_ffn_G2_I"])
    shared["kron_AR"] = np.stack(AsR); shared["kron_BR"] = np.stack(BsR)
    shared["kron_AI"] = np.stack(AsI); shared["kron_BI"] = np.stack(BsI)
    shared["ffn_bias"] = np.concatenate([inp["L0_ffn_bias_real"], inp["L0_ffn_bias_imag"]]).reshape(8, P).T.copy()

    perm = np.concatenate([np.arange(0, DM, 2), np.arange(1, DM, 2)])
    for nm in ["wq", "wk", "wv"]:
        w = inp[f"L1_attn_{nm}"]
        shared[nm + "_t"] = np.ascontiguousarray(w[perm, :].T)
    ipw = inp["L1_attn_in_proj_w"]
    ipb = inp["L1_attn_in_proj_b"]
    wqi, wki, wvi = np.split(ipw, 3, 0)
    shared["wqi_t"] = np.ascontiguousarray(wqi[:, perm].T)
    shared["wki_t"] = np.ascontiguousarray(wki[:, perm].T)
    shared["wvi_t"] = np.ascontiguousarray(wvi[:, perm].T)
    bqi, bki, bvi = np.split(ipb, 3, 0)
    shared["bqi"] = bqi.reshape(8, P).T.copy()
    shared["bki"] = bki.reshape(8, P).T.copy()
    shared["bvi"] = bvi.reshape(1, DM)
    shared["out_w_t"] = np.ascontiguousarray(inp["L1_attn_out_w"].T)
    shared["out_b"] = inp["L1_attn_out_b"].reshape(8, P).T.copy()

    shared["Wg"] = inp["L1_ffn_Wg"]
    shared["W1"] = inp["L1_ffn_W1"]
    shared["b1"] = np.ascontiguousarray(inp["L1_ffn_b1"].reshape(NEXP, 32, P).transpose(0, 2, 1))
    shared["W2"] = inp["L1_ffn_W2"]
    shared["b2"] = inp["L1_ffn_b2"]

    x = inp["L0_attn_x"]
    in_maps = []
    for c in range(NCORES):
        m = dict(shared)
        m["x_in"] = np.ascontiguousarray(x[2 * c:2 * c + 2])
        cm = np.zeros((8, 1), dtype=f32); cm[:c, 0] = 1.0
        m["c_coremask"] = cm
        in_maps.append(m)
    return in_maps


class Plan:
    def __init__(self):
        self.items = []

    def em(self, engine, fn):
        self.items.append((engine, fn))

    def replay(self, engine, eng):
        for e, fn in self.items:
            if e == engine:
                fn(eng)


class Ctr:
    def __init__(self, sem):
        self.sem = sem
        self.v = 0

    def inc(self, n=1):
        self.v += n
        return self.v


ENGS = ["sync", "tensor", "vector", "scalar", "gpsimd"]


def barrier(plan, ctr):
    target = ctr.inc(len(ENGS))
    for e in ENGS:
        def fn(eng, t=target):
            eng.nop().then_inc(ctr.sem, 1)
            eng.wait_ge(ctr.sem, t)
        plan.em(e, fn)


def build_nc(debug_outputs=(), stop_after=None):
    nc = bass.Bass("TRN2", target_bir_lowering=False)
    g = {}

    def din(name, shape, dt=F32):
        g[name] = nc.dram_tensor(name, list(shape), dt, kind="ExternalInput").ap()

    def dscr(name, shape, dt=F32):
        kind = "ExternalOutput" if name in debug_outputs else "Internal"
        g[name] = nc.dram_tensor(name, list(shape), dt, kind=kind).ap()

    din("x_in", [NSEQ, LSEQ, DM])
    din("c_ident_b", [P, P]); din("c_ident_f", [P, P]); din("c_ones_row", [1, P])
    din("c_lt_strict", [P, P]); din("c_stat_sx", [P, 2]); din("c_stat_sq", [P, 2])
    din("c_ones_col", [P, 1])
    din("c_diagmask", [4, P, 512]); din("c_rope_cos", [512, LSEQ]); din("c_rope_sin", [512, LSEQ])
    din("c_iota8", [P, 8]); din("c_coremask", [8, 1]); din("c_zero_row", [1, DM])
    din("c_rowsel32", [32, 32 * P]); din("c_rowsel16", [16, 16 * P]); din("c_rowsel2", [2, 2 * P])
    din("w_in", [DM, 2 * DI]); din("norm_w", [P, 8])
    din("conv_diag", [4, 16, P, P]); din("conv_b", [P, 16])
    din("w_x", [DI, 96]); din("w_dt", [DTR, DI]); din("b_dt", [P, 16])
    din("A_log", [P, 256]); din("Dp", [P, 16]); din("w_out", [DI, DM])
    din("kron_AR", [4, 512, 512]); din("kron_BR", [4, 512, 512])
    din("kron_AI", [4, 512, 512]); din("kron_BI", [4, 512, 512]); din("ffn_bias", [P, 8])
    din("wq_t", [DM, DM]); din("wk_t", [DM, DM]); din("wv_t", [DM, DM])
    din("wqi_t", [DM, DM]); din("wki_t", [DM, DM]); din("wvi_t", [DM, DM])
    din("bqi", [P, 8]); din("bki", [P, 8]); din("bvi", [1, DM])
    din("out_w_t", [DM, DM]); din("out_b", [P, 8])
    din("Wg", [DM, NEXP]); din("W1", [NEXP, DM, 4 * DM]); din("b1", [NEXP, P, 32])
    din("W2", [NEXP, 4 * DM, DM]); din("b2", [NEXP, DM])

    g["out"] = nc.dram_tensor("out", [NSEQ, LSEQ, DM], F32, kind="ExternalOutput").ap()

    dscr("xs", [DM, TPC])
    dscr("sc_u", [DI, TPC], BF16)
    dscr("sc_delta", [DI, TPC])
    dscr("sc_du", [DI, TPC])
    dscr("sc_gate", [DI, TPC], BF16)
    dscr("sc_q", [DM, TPC], BF16)
    dscr("sc_k", [DM, TPC], BF16)
    dscr("sc_v", [DM, TPC], BF16)
    dscr("sc_qh", [DM, TPC], BF16)
    dscr("sc_kh", [DM, TPC], BF16)
    dscr("sc_vh_tm", [TPC, DM], BF16)
    dscr("sc_o", [DM, TPC], BF16)
    dscr("sc_xn4_tm", [TPC + 1, DM], BF16)
    dscr("sc_x3_tm", [TPC, DM])
    dscr("sc_dbg", [P, 2048])
    dscr("sc_dbg2", [2, TPC])
    dscr("sc_gh", [4 * DM, CAPL], BF16)
    g["agin"] = nc.dram_tensor("agin", [1, NEXP], F32, kind="Internal").ap()
    g["agout"] = nc.dram_tensor("agout", [NEXP, NEXP], F32, kind="Internal").ap()
    dscr("sc_ein", [NEXP * CAPL + 1, DM], BF16)
    dscr("sc_eout", [NEXP * CAPL + 1, DM], BF16)

    plan = Plan()
    ctx = ExitStack()

    sem_handles = {}
    for nm in ["bar", "s_ld", "s_lg", "s_mm", "s_ve", "s_ac", "s_gp", "s_st", "s_x1", "s_x2"]:
        sem_handles[nm] = ctx.enter_context(nc.semaphore(nm))
    C = {nm: Ctr(h) for nm, h in sem_handles.items()}

    st = _State(nc, g, plan, C, ctx)
    st.emit_all(stop_after=stop_after)

    with nc.Block() as block:
        @block.sync
        def _(e): plan.replay("sync", e)
        @block.tensor
        def _(e): plan.replay("tensor", e)
        @block.vector
        def _(e): plan.replay("vector", e)
        @block.scalar
        def _(e): plan.replay("scalar", e)
        @block.gpsimd
        def _(e): plan.replay("gpsimd", e)

    ctx.close()
    return nc


class _State:

    def __init__(self, nc, g, plan, C, ctx):
        self.nc = nc
        self.g = g
        self.plan = plan
        self.C = C
        self.ctx = ctx
        self.consts = {}

    def sb(self, name, shape, dt=F32):
        return self.ctx.enter_context(self.nc.sbuf_tensor(name, list(shape), dt))

    def ps(self, name, shape, dt=F32):
        return self.ctx.enter_context(self.nc.psum_tensor(name, list(shape), dt))

    def load(self, dst_ap, src_ap, cast=False, wait=()):
        name = "s_lg" if cast else "s_ld"
        C = self.C[name]
        t = C.inc(16)
        eng = "gpsimd" if cast else "sync"
        def fn(e, d=dst_ap, s=src_ap, w=tuple(wait), p=t - 16):
            e.wait_ge(C.sem, p)
            for sem, v in w:
                e.wait_ge(sem, v)
            e.dma_start(out=d, in_=s).then_inc(C.sem, 16)
        self.plan.em(eng, fn)
        return (name, t)

    def store(self, dst_ap, src_ap, wait=()):
        C = self.C["s_st"]
        t = C.inc(16)
        def fn(e, d=dst_ap, s=src_ap, w=tuple(wait), p=t - 16):
            e.wait_ge(C.sem, p)
            for sem, v in w:
                e.wait_ge(sem, v)
            e.dma_start(out=d, in_=s).then_inc(C.sem, 16)
        self.plan.em("sync", fn)
        return t

    def wait(self, engine, ctr_name, val):
        if isinstance(ctr_name, tuple):
            ctr_name, val = ctr_name
        elif isinstance(val, tuple):
            ctr_name, val = val
        sem = self.C[ctr_name].sem
        self.plan.em(engine, lambda e, s=sem, v=val: e.wait_ge(s, v))

    def emit(self, engine, fn, inc=None):
        if inc is None:
            self.plan.em(engine, fn)
            return None
        C = self.C[inc]
        t = C.inc(1)
        def wrapped(e, f=fn, s=C.sem):
            f(e).then_inc(s, 1)
        self.plan.em(engine, wrapped)
        return t

    def barrier(self):
        barrier(self.plan, self.C["bar"])

    def emit_all(self, stop_after=None):
        phases = [self.p_consts, self.p0_transpose_in, self.p1_mamba, self.p2_ffn,
                  self.p3_attn, self.p4_moe]
        names = ["pc", "p0", "p1", "p2", "p3", "p4"]
        for nm, ph in zip(names, phases):
            ph()
            self.barrier()
            if stop_after == nm:
                break
        stv = self.C["s_st"].v
        ssem = self.C["s_st"].sem
        self.plan.em("sync", lambda e, v=stv, s=ssem: e.wait_ge(s, v))

    def p_consts(self):
        g = self.g
        cs = self.consts
        cs["ident_b"] = self.sb("ident_b", [P, P], BF16)
        cs["ident_f"] = self.sb("ident_f", [P, P], F32)
        cs["ones_row_b"] = self.sb("ones_row_b", [1, P], BF16)
        cs["ones_row_f"] = self.sb("ones_row_f", [1, P], F32)
        cs["ones_col_f"] = self.sb("ones_col_f", [P, 1], F32)
        cs["lt_b"] = self.sb("lt_b", [P, P], BF16)
        cs["stat_sx"] = self.sb("stat_sx", [P, 2], BF16)
        cs["stat_sq"] = self.sb("stat_sq", [P, 2], BF16)
        cs["diagmask"] = self.sb("diagmask", [P, 4 * 512], BF16)
        cs["iota8"] = self.sb("iota8", [P, 8], F32)
        cs["rowsel32_b"] = self.sb("rowsel32_b", [32, 32 * P], BF16)
        cs["rowsel16_f"] = self.sb("rowsel16_f", [16, 16 * P], F32)
        cs["rowsel2_f"] = self.sb("rowsel2_f", [2, 2 * P], F32)
        cs["coremask"] = self.sb("coremask", [8, 1], F32)
        self.load(cs["ident_b"][:, :], g["c_ident_b"][:, :], cast=True)
        self.load(cs["ident_f"][:, :], g["c_ident_f"][:, :])
        self.load(cs["ones_row_b"][:, :], g["c_ones_row"][:, :], cast=True)
        self.load(cs["ones_row_f"][:, :], g["c_ones_row"][:, :])
        self.load(cs["ones_col_f"][:, :], g["c_ones_col"][:, :])
        self.load(cs["lt_b"][:, :], g["c_lt_strict"][:, :], cast=True)
        self.load(cs["stat_sx"][:, :], g["c_stat_sx"][:, :], cast=True)
        self.load(cs["stat_sq"][:, :], g["c_stat_sq"][:, :], cast=True)
        for i in range(4):
            self.load(cs["diagmask"][:, 512 * i:512 * (i + 1)], g["c_diagmask"][i, :, :], cast=True)
        self.load(cs["iota8"][:, :], g["c_iota8"][:, :])
        self.load(cs["rowsel32_b"][:, :], g["c_rowsel32"][:, :], cast=True)
        self.load(cs["rowsel16_f"][:, :], g["c_rowsel16"][:, :])
        self.load(cs["rowsel2_f"][:, :], g["c_rowsel2"][:, :])
        self.load(cs["coremask"][:, :], g["c_coremask"][:, :])

    def p0_transpose_in(self):
        g, nc = self.g, self.nc
        with ExitStack() as px:
            tin = px.enter_context(nc.sbuf_tensor("p0_tin", [P, P], F32))
            tev = px.enter_context(nc.sbuf_tensor("p0_tev", [P, 16 * P], F32))
            pst = px.enter_context(nc.psum_tensor("p0_ps", [P, P], F32))
            for seq in range(NSEQ):
                for dc in range(8):
                    for tt in range(16):
                        lv = self.load(tin[:, :], g["x_in"][seq, tt * P:(tt + 1) * P, dc * P:(dc + 1) * P],
                                       wait=[(self.C["s_mm"].sem, self.C["s_mm"].v)])
                        self.wait("tensor", "s_ld", lv)
                        mv = self.emit("tensor", lambda e, o=pst[:, :], i=tin[:, :]:
                                       e.transpose(o, i, self.consts["ident_f"][:, :]), inc="s_mm")
                        self.wait("vector", "s_mm", mv)
                        vv = self.emit("vector", lambda e, o=tev[:, tt * P:(tt + 1) * P], i=pst[:, :]:
                                       e.tensor_copy(o, i), inc="s_ve")
                        self.wait("tensor", "s_ve", vv)
                    sv = self.C["s_ve"].v
                    self.store(g["xs"][dc * P:(dc + 1) * P, seq * LSEQ:(seq + 1) * LSEQ], tev[:, :],
                               wait=[(self.C["s_ve"].sem, sv)])
                    self.wait("vector", "s_st", self.C["s_st"].v)

    def mm(self, ps_ap, stat_ap, mov_ap, start, stop, inc=None):
        def fn(e, o=ps_ap, s=stat_ap, m=mov_ap, st=start, sp=stop):
            return e.matmul(o, s, m, start=st, stop=sp)
        return self.emit("tensor", fn, inc=inc)

    def act(self, out, in_, func, scale=None, bias=None, inc=None, engine="scalar"):
        kw = {}
        if scale is not None:
            kw["scale"] = scale
        if bias is not None:
            kw["bias"] = bias
        def fn(e, o=out, i=in_, f=func, k=kw):
            return e.activation(out=o, in_=i, func=f, **k)
        return self.emit(engine, fn, inc=inc)

    def tt(self, out, a, b, op, inc=None, engine="vector"):
        def fn(e, o=out, x=a, y=b, p=op):
            return e.tensor_tensor(o, x, y, op=p)
        t = self.emit(engine, fn, inc="s_ve")
        self.wait(engine, "s_ve", t)
        return t

    def stt(self, out, in0, scalar, in1, op0, op1, inc=None, engine="vector"):
        def fn(e, o=out, i0=in0, s=scalar, i1=in1, p0=op0, p1=op1):
            return e.scalar_tensor_tensor(o, i0, s, i1, op0=p0, op1=p1)
        t = self.emit(engine, fn, inc="s_ve")
        self.wait(engine, "s_ve", t)
        return t

    def tscal(self, out, in_, s1, s2, op0, op1=None, inc=None, engine="vector"):
        def fn(e, o=out, i=in_, a=s1, b=s2, p0=op0, p1=op1):
            if p1 is None:
                return e.tensor_scalar(o, i, a, None, op0=p0)
            return e.tensor_scalar(o, i, a, b, op0=p0, op1=p1)
        t = self.emit(engine, fn, inc="s_ve")
        self.wait(engine, "s_ve", t)
        return t

    def vcopy(self, out, in_, inc=None, engine="vector"):
        t = self.emit(engine, lambda e, o=out, i=in_: e.tensor_copy(o, i), inc="s_ve")
        self.wait(engine, "s_ve", t)
        return t

    def vmemset(self, ap, val, inc=None, engine="vector"):
        t = self.emit(engine, lambda e, a=ap, v=val: e.memset(a, v), inc="s_ve")
        self.wait(engine, "s_ve", t)
        return t

    def scan_op(self, out, a, b, initial, inc=None):
        def fn(e, o=out, x=a, y=b, i=initial):
            return e.tensor_tensor_scan(o, x, y, i, op0=OP.mult, op1=OP.add)
        t = self.emit("vector", fn, inc="s_ve")
        self.wait("vector", "s_ve", t)
        return t

    def recip(self, out, in_, inc=None):
        t = self.emit("vector", lambda e, o=out, i=in_: e.reciprocal(o, i), inc="s_ve")
        self.wait("vector", "s_ve", t)
        return t

    def vreduce(self, out, in_, op=OP.add):
        t = self.emit("vector", lambda e, o=out, i=in_, p=op:
                      e.tensor_reduce(o, i, axis=mybir.AxisListType.X, op=p), inc="s_ve")
        self.wait("vector", "s_ve", t)
        return t

    def transpose_mm(self, ps_ap, in_ap, dtype_b, inc=None):
        ident = self.consts["ident_b" if dtype_b else "ident_f"]
        def fn(e, o=ps_ap, i=in_ap, idn=ident[:, :]):
            return e.transpose(o, i, idn)
        return self.emit("tensor", fn, inc=inc)

    def rep_row(self, ps_ap, row_ap, inc=None):
        return self.mm(ps_ap, self.consts["ones_row_f"][:, :], row_ap, True, True, inc=inc)

    def rep_stat(self, ps_ap, block16_ap, r, inc=None):
        sel = self.consts["rowsel16_f"][:, r * P:(r + 1) * P]
        return self.mm(ps_ap, sel, block16_ap, True, True, inc=inc)

    def rep_row32(self, ps_ap, block32_ap, r, inc=None):
        sel = self.consts["rowsel32_b"][:, r * P:(r + 1) * P]
        return self.mm(ps_ap, sel, block32_ap, True, True, inc=inc)

    def ln_chain(self, stats_ap, c0, w, mode):
        sel = self.consts["rowsel2_f"]
        self.mm(w["ps_m"][:, :], sel[:, 0:P], stats_ap[0:2, c0:c0 + 512], True, True)
        mv = self.mm(w["ps_e"][:, :], sel[:, P:2 * P], stats_ap[0:2, c0:c0 + 512], True, True, inc="s_mm")
        self.wait("vector", "s_mm", mv)
        self.tscal(w["m"][:, :], w["ps_m"][:, :], 1.0 / DM, None, OP.mult)
        self.tscal(w["t1"][:, :], w["ps_e"][:, :], 1.0 / DM, None, OP.mult)
        self.tt(w["v"][:, :], w["m"][:, :], w["m"][:, :], OP.mult)
        self.tt(w["v"][:, :], w["t1"][:, :], w["v"][:, :], OP.subtract)
        vv = self.tscal(w["t1"][:, :], w["v"][:, :], 1e-5, None, OP.add, inc="s_ve")
        self.wait("scalar", "s_ve", vv)
        av = self.act(w["t1"][:, :], w["t1"][:, :], AF.Sqrt, inc="s_ac")
        self.wait("vector", "s_ac", av)
        self.recip(w["t1"][:, :], w["t1"][:, :])
        if mode == "rms":
            self.tt(w["t2"][:, :], w["v"][:, :], w["t1"][:, :], OP.mult)
            self.tt(w["t2"][:, :], w["t2"][:, :], w["t1"][:, :], OP.mult)
            vv = self.tscal(w["t2"][:, :], w["t2"][:, :], EPS32, None, OP.add, inc="s_ve")
            self.wait("scalar", "s_ve", vv)
            av = self.act(w["t2"][:, :], w["t2"][:, :], AF.Sqrt, inc="s_ac")
            self.wait("vector", "s_ac", av)
            self.recip(w["t2"][:, :], w["t2"][:, :])
            self.tt(w["o1"][:, :], w["t1"][:, :], w["t2"][:, :], OP.mult)
            vv = self.tt(w["o2"][:, :], w["m"][:, :], w["o1"][:, :], OP.mult, inc="s_ve")
        elif mode == "ln":
            self.vcopy(w["o1"][:, :], w["t1"][:, :])
            vv = self.tt(w["o2"][:, :], w["m"][:, :], w["t1"][:, :], OP.mult, inc="s_ve")
        else:
            self.tscal(w["o1"][:, :], w["t1"][:, :], 1.0, None, OP.add)
            vv = self.tt(w["o2"][:, :], w["m"][:, :], w["t1"][:, :], OP.mult, inc="s_ve")
        return vv

    def stats_pass(self, x_t, xq_t, ps_st):
        g, C = self.g, self.C
        for tc in range(TPC // 512):
            for dc in range(8):
                sl = (dc % 2) * 512
                lvx = self.load(x_t[:, sl:sl + 512],
                                g["xs"][dc * P:(dc + 1) * P, tc * 512:(tc + 1) * 512],
                                cast=True, wait=[(C["s_mm"].sem, C["s_mm"].v)])
                self.wait("scalar", "s_ld", lvx)
                avq = self.act(xq_t[:, :], x_t[:, sl:sl + 512], AF.Square, inc="s_ac")
                self.wait("tensor", "s_ac", avq)
                self.wait("tensor", "s_ld", lvx)
                self.mm(ps_st[0:2, 0:512], self.consts["stat_sx"][:, :], x_t[:, sl:sl + 512], dc == 0, False)
                mmv = self.mm(ps_st[0:2, 0:512], self.consts["stat_sq"][:, :], xq_t[:, :], False, dc == 7, inc="s_mm")
                self.wait("scalar", "s_mm", mmv)
            self.wait("vector", "s_mm", mmv)
            vv = self.vcopy(self.stats[0:2, tc * 512:(tc + 1) * 512], ps_st[0:2, 0:512], inc="s_ve")
            self.wait("tensor", "s_ve", vv)
        return vv

    def p1_mamba(self):
        g, nc, C = self.g, self.nc, self.C

        normw = self.sb("normw", [P, 8])
        convb = self.sb("convb", [P, 16])
        nb2 = self.sb("nb2", [P, 16])
        bdt = self.sb("bdt", [P, 16])
        Asc = self.sb("Asc", [P, 256])
        Dp = self.sb("Dp_sb", [P, 16])
        stats = self.sb("stats", [2, TPC])
        self.stats = stats
        BCb = self.sb("BCb", [32, TPC], BF16)
        hstate = self.sb("hstate", [P, 256 * NSEQ])

        lv = self.load(normw[:, :], g["norm_w"][:, :])
        lv = self.load(convb[:, :], g["conv_b"][:, :])
        lv = self.load(bdt[:, :], g["b_dt"][:, :])
        lv = self.load(Dp[:, :], g["Dp"][:, :])
        alog_t = self.sb("alog_t", [P, 256])
        lv = self.load(alog_t[:, :], g["A_log"][:, :])
        self.wait("vector", "s_ld", lv)
        vv = self.tscal(nb2[:, :], bdt[:, :], -2.0, None, OP.mult, inc="s_ve")
        self.wait("scalar", "s_ve", vv)
        self.wait("scalar", "s_ld", lv)
        av = self.act(Asc[:, :], alog_t[:, :], AF.Exp, inc="s_ac")
        self.wait("vector", "s_ac", av)
        vv = self.tscal(Asc[:, :], Asc[:, :], -1.0, None, OP.mult, inc="s_ve")
        self.wait("scalar", "s_ve", vv)

        with ExitStack() as px:
            sbl = lambda n, s, d=F32: px.enter_context(nc.sbuf_tensor(n, list(s), d))
            w_in_bf = sbl("w_in_bf", [P, 8 * 4096], BF16)
            w_x_bf = sbl("w_x_bf", [P, 16 * 96], BF16)
            w_dt_bf = sbl("w_dt_bf", [DTR, DI], BF16)
            cdiag = sbl("cdiag", [P, 4 * 16 * P], BF16)
            x_t = sbl("x_t", [P, 2 * 512], BF16)
            xq_t = sbl("xq_t", [P, 512], BF16)
            lnw = {}
            for nm in ["m", "v", "t1", "t2", "o1", "o2"]:
                lnw[nm] = sbl("lnw_" + nm, [P, 512])
            xrb = sbl("xrb", [P, 8 * 512], BF16)
            xssm_ext = sbl("xssm_ext", [P, 16 * 515], BF16)
            u_all = sbl("u_all", [P, 16 * 512], BF16)
            dtl_t = sbl("dtl_t", [DTR, 512], BF16)
            sg_t = sbl("sg_t", [P, 512])
            l_t = sbl("l_t", [P, 512])
            dl_t = sbl("dl_t", [P, 512])
            du_t = sbl("du_t", [P, 512])
            ps_xp = px.enter_context(nc.psum_tensor("ps_xp", [P, 2 * 512], F32))
            ps_cv = px.enter_context(nc.psum_tensor("ps_cv", [P, 512], F32))
            ps_db = px.enter_context(nc.psum_tensor("ps_db", [DTR, 512], F32))
            ps_bc = px.enter_context(nc.psum_tensor("ps_bc", [32, 512], F32))
            ps_dt = px.enter_context(nc.psum_tensor("ps_dt", [P, 512], F32))
            ps_r1 = px.enter_context(nc.psum_tensor("ps_r1", [P, 512], F32))
            ps_r2 = px.enter_context(nc.psum_tensor("ps_r2", [P, 512], F32))
            ps_st = ps_r1

            for dc in range(8):
                self.load(w_in_bf[:, dc * 4096:(dc + 1) * 4096], g["w_in"][dc * P:(dc + 1) * P, :], cast=True)
            for dc in range(16):
                self.load(w_x_bf[:, dc * 96:(dc + 1) * 96], g["w_x"][dc * P:(dc + 1) * P, :], cast=True)
            lv = self.load(w_dt_bf[:, :], g["w_dt"][:, :], cast=True)
            for k in range(4):
                for dc in range(16):
                    lv = self.load(cdiag[:, (k * 16 + dc) * P:(k * 16 + dc + 1) * P],
                                   g["conv_diag"][k, dc, :, :], cast=True)
            wloads = C["s_lg"].v
            self.wait("vector", "s_lg", wloads)
            for dc in range(8):
                vv = self.tt(w_in_bf[:, dc * 4096:(dc + 1) * 4096],
                             w_in_bf[:, dc * 4096:(dc + 1) * 4096],
                             normw[:, dc:dc + 1].to_broadcast([P, 4096]), OP.mult, inc="s_ve")
            w_ready_v = vv

            self.stats_pass(x_t, xq_t, ps_st)

            for seq in range(NSEQ):
                for tcl in range(4):
                    tc = seq * 4 + tcl
                    c0 = tc * 512
                    wln = dict(lnw); wln["ps_m"] = ps_r1; wln["ps_e"] = ps_r2
                    chv = self.ln_chain(stats, c0, wln, "rms")
                    self.wait("scalar", "s_ve", C["s_ve"].v)
                    for dc in range(8):
                        sl = (dc % 2) * 512
                        lvx = self.load(x_t[:, sl:sl + 512],
                                        g["xs"][dc * P:(dc + 1) * P, c0:c0 + 512],
                                        cast=True,
                                        wait=[(C["s_ve"].sem, chv if dc < 2 else vv)])
                        self.wait("vector", "s_ld", lvx)
                        self.tt(xrb[:, dc * 512:(dc + 1) * 512], x_t[:, sl:sl + 512],
                                lnw["o1"][:, :], OP.mult)
                        vv = self.tt(xrb[:, dc * 512:(dc + 1) * 512], xrb[:, dc * 512:(dc + 1) * 512],
                                     lnw["o2"][:, :], OP.subtract, inc="s_ve")
                    self.wait("tensor", "s_ve", vv)
                    if tc == 0:
                        self.wait("tensor", "s_ve", w_ready_v)

                    if tcl == 0:
                        for oc in range(16):
                            self.vmemset(xssm_ext[:, oc * 515:oc * 515 + 3], 0.0, engine="gpsimd")
                    else:
                        for oc in range(16):
                            self.vcopy(xssm_ext[:, oc * 515:oc * 515 + 3],
                                       xssm_ext[:, oc * 515 + 512:oc * 515 + 515], engine="gpsimd")
                    gv_tail = self.emit("gpsimd", lambda e: e.engine_nop(), inc="s_gp")

                    for oc in range(32):
                        pxp = ps_xp[:, (oc % 2) * 512:(oc % 2) * 512 + 512]
                        for dc in range(8):
                            stat = w_in_bf[:, dc * 4096 + oc * P:dc * 4096 + (oc + 1) * P]
                            mv = self.mm(pxp, stat, xrb[:, dc * 512:(dc + 1) * 512], dc == 0, dc == 7,
                                         inc=("s_mm" if dc == 7 else None))
                        if oc < 16:
                            if oc == 0:
                                self.wait("scalar", "s_gp", gv_tail)
                            self.wait("scalar", "s_mm", mv)
                            av = self.act(xssm_ext[:, oc * 515 + 3:oc * 515 + 515], pxp, AF.Copy, inc="s_ac")
                        else:
                            self.wait("scalar", "s_mm", mv)
                            av = self.act(u_all[:, (oc - 16) * 512:(oc - 15) * 512], pxp, AF.Silu, inc="s_ac")
                            self.store(g["sc_gate"][(oc - 16) * P:(oc - 15) * P, c0:c0 + 512],
                                       u_all[:, (oc - 16) * 512:(oc - 15) * 512],
                                       wait=[(C["s_ac"].sem, av)])
                        self.wait("tensor", "s_ac", av)
                    self.wait("scalar", "s_st", C["s_st"].v)

                    for oc in range(16):
                        for k in range(4):
                            stat = cdiag[:, (k * 16 + oc) * P:(k * 16 + oc + 1) * P]
                            mov = xssm_ext[:, oc * 515 + k:oc * 515 + k + 512]
                            mv = self.mm(ps_cv[:, :], stat, mov, k == 0, k == 3,
                                         inc=("s_mm" if k == 3 else None))
                        self.wait("scalar", "s_mm", mv)
                        av = self.act(u_all[:, oc * 512:(oc + 1) * 512], ps_cv[:, :], AF.Silu,
                                      bias=convb[:, oc:oc + 1], inc="s_ac")
                        self.store(g["sc_u"][oc * P:(oc + 1) * P, c0:c0 + 512],
                                   u_all[:, oc * 512:(oc + 1) * 512], wait=[(C["s_ac"].sem, av)])
                        self.wait("tensor", "s_ac", av)

                    for dc in range(16):
                        self.mm(ps_db[:, :], w_x_bf[:, dc * 96:dc * 96 + DTR],
                                u_all[:, dc * 512:(dc + 1) * 512], dc == 0, dc == 15)
                        mv = self.mm(ps_bc[:, :], w_x_bf[:, dc * 96 + DTR:(dc + 1) * 96],
                                     u_all[:, dc * 512:(dc + 1) * 512], dc == 0, dc == 15,
                                     inc=("s_mm" if dc == 15 else None))
                    self.wait("vector", "s_mm", mv)
                    self.vcopy(dtl_t[:, :], ps_db[0:DTR, :])
                    vv = self.vcopy(BCb[0:32, c0:c0 + 512], ps_bc[0:32, :], inc="s_ve")
                    self.wait("tensor", "s_ve", vv)

                    for oc in range(16):
                        mv = self.mm(ps_dt[:, :], w_dt_bf[:, oc * P:(oc + 1) * P], dtl_t[:, :],
                                     True, True, inc="s_mm")
                        self.wait("scalar", "s_mm", mv)
                        self.act(sg_t[:, :], ps_dt[:, :], AF.Sigmoid, scale=-1.0, bias=nb2[:, oc:oc + 1])
                        av = self.act(l_t[:, :], sg_t[:, :], AF.Ln, inc="s_ac")
                        self.wait("tensor", "s_ac", av)
                        self.wait("vector", "s_ac", av)
                        self.tscal(dl_t[:, :], l_t[:, :], -1.0, None, OP.mult)
                        vv = self.tt(du_t[:, :], dl_t[:, :], u_all[:, oc * 512:(oc + 1) * 512],
                                     OP.mult, inc="s_ve")
                        self.store(g["sc_delta"][oc * P:(oc + 1) * P, c0:c0 + 512], dl_t[:, :],
                                   wait=[(C["s_ve"].sem, vv)])
                        self.store(g["sc_du"][oc * P:(oc + 1) * P, c0:c0 + 512], du_t[:, :],
                                   wait=[(C["s_ve"].sem, vv)])
                        self.wait("vector", "s_st", C["s_st"].v)

        self.barrier()
        with ExitStack() as px:
            sbl = lambda n, s, d=F32: px.enter_context(nc.sbuf_tensor(n, list(s), d))
            w_out_bf = sbl("w_out_bf", [P, 16 * DM], BF16)
            Brep = sbl("Brep", [P, 16 * 1024], BF16)
            Crep = sbl("Crep", [P, 16 * 1024], BF16)
            delta_t = sbl("delta_t", [P, 1024], BF16)
            duu_t = sbl("duu_t", [P, 1024], BF16)
            arena = sbl("p1e_arena", [P, 3584])
            dA_t = arena[:, 0:1024]
            beta_t = arena[:, 1024:2048]
            h_t = sbl("h_t", [P, 1024], BF16)
            ht_t = sbl("ht_t", [P, 1024], BF16)
            u_t = sbl("u_t", [P, 1024], BF16)
            gt_t = sbl("gt_t", [P, 1024], BF16)

            xf_all = sbl("xf_all", [P, 16 * 1024], BF16)
            ys_t = arena[:, 3072:3584].bitcast(BF16)
            x_t2 = arena[:, 3072:3584]
            lnw2 = {nm: arena[:, i * 512:(i + 1) * 512]
                    for i, nm in enumerate(["m", "v", "t1", "t2", "o1", "o2"])}
            ps_rep = px.enter_context(nc.psum_tensor("ps_rep", [P, 512], F32))
            ps_y = px.enter_context(nc.psum_tensor("ps_y", [P, 1024], F32))
            ps_wo = px.enter_context(nc.psum_tensor("ps_wo", [P, 512], F32))
            ps_ra = px.enter_context(nc.psum_tensor("ps_ra", [P, 512], F32))
            ps_rb = px.enter_context(nc.psum_tensor("ps_rb", [P, 512], F32))

            for dc in range(16):
                lv = self.load(w_out_bf[:, dc * DM:(dc + 1) * DM], g["w_out"][dc * P:(dc + 1) * P, :], cast=True)
            self.wait("tensor", "s_ld", lv)

            for seq in range(NSEQ):
                for th in range(2):
                    hb0 = seq * LSEQ + th * 1024
                    for n in range(NST):
                        for j in range(2):
                            mv = self.rep_row32(ps_rep[:, :],
                                                BCb[0:32, hb0 + j * 512:hb0 + (j + 1) * 512],
                                                n, inc="s_mm")
                            self.wait("vector", "s_mm", mv)
                            vv = self.vcopy(Brep[:, n * 1024 + j * 512:n * 1024 + (j + 1) * 512],
                                            ps_rep[:, :], inc="s_ve")
                            self.wait("tensor", "s_ve", vv)
                            mv = self.rep_row32(ps_rep[:, :],
                                                BCb[0:32, hb0 + j * 512:hb0 + (j + 1) * 512],
                                                16 + n, inc="s_mm")
                            self.wait("vector", "s_mm", mv)
                            vv = self.vcopy(Crep[:, n * 1024 + j * 512:n * 1024 + (j + 1) * 512],
                                            ps_rep[:, :], inc="s_ve")
                            self.wait("tensor", "s_ve", vv)

                    for dk in range(16):
                        self.wait("gpsimd", "s_ve", C["s_ve"].v)
                        lv = self.load(delta_t[:, :], g["sc_delta"][dk * P:(dk + 1) * P, hb0:hb0 + 1024],
                                       cast=True)
                        lv = self.load(duu_t[:, :], g["sc_du"][dk * P:(dk + 1) * P, hb0:hb0 + 1024],
                                       cast=True)
                        self.load(u_t[:, :], g["sc_u"][dk * P:(dk + 1) * P, hb0:hb0 + 1024],
                                  wait=[(C["s_ve"].sem, C["s_ve"].v)])
                        lvg = self.load(gt_t[:, :], g["sc_gate"][dk * P:(dk + 1) * P, hb0:hb0 + 1024])
                        self.wait("scalar", "s_ld", lv)
                        self.wait("vector", "s_ld", lvg)
                        for n in range(NST):
                            av = self.act(dA_t, delta_t[:, :], AF.Exp,
                                          scale=Asc[:, dk * 16 + n:dk * 16 + n + 1], inc="s_ac")
                            self.wait("vector", "s_ac", av)
                            self.tt(beta_t, duu_t[:, :],
                                    Brep[:, n * 1024:(n + 1) * 1024], OP.mult)
                            if th == 0:
                                ini = 0.0
                            else:
                                ini = hstate[:, seq * 256 + dk * 16 + n:seq * 256 + dk * 16 + n + 1]
                            scv = self.scan_op(h_t[:, :], dA_t, beta_t, ini, inc="s_ve")
                            if th == 0:
                                self.vcopy(hstate[:, seq * 256 + dk * 16 + n:seq * 256 + dk * 16 + n + 1],
                                           h_t[:, 1023:1024])
                            vv = self.tt(ht_t[:, :], h_t[:, :], Crep[:, n * 1024:(n + 1) * 1024],
                                         OP.mult, inc="s_ve")
                            self.wait("tensor", "s_ve", vv)

                            for j in range(2):
                                mv = self.mm(ps_y[:, j * 512:(j + 1) * 512], self.consts["ident_b"][:, :],
                                             ht_t[:, j * 512:(j + 1) * 512], n == 0, n == NST - 1,
                                             inc=("s_mm" if j == 1 else None))
                            if n < NST - 1:
                                self.wait("vector", "s_mm", mv)
                                self.wait("scalar", "s_ve", vv)
                        self.wait("vector", "s_mm", mv)
                        self.stt(ys_t, u_t[:, :], Dp[:, dk:dk + 1], ps_y[:, :],
                                 OP.mult, OP.add)
                        vv = self.tt(xf_all[:, dk * 1024:(dk + 1) * 1024], ys_t, gt_t[:, :],
                                     OP.mult, inc="s_ve")
                        self.wait("tensor", "s_ve", vv)


                    for j in range(2):
                        c0 = hb0 + j * 512
                        wln = dict(lnw2); wln["ps_m"] = ps_ra; wln["ps_e"] = ps_rb
                        chv = self.ln_chain(stats, c0, wln, "res")
                        for oc in range(8):
                            for dk in range(16):
                                stat = w_out_bf[:, dk * DM + oc * P:dk * DM + (oc + 1) * P]
                                mv = self.mm(ps_wo[:, :], stat, xf_all[:, dk * 1024 + j * 512:dk * 1024 + (j + 1) * 512],
                                             dk == 0, dk == 15, inc=("s_mm" if dk == 15 else None))
                            lvx = self.load(x_t2[:, :], g["xs"][oc * P:(oc + 1) * P, c0:c0 + 512],
                                            wait=[(C["s_ve"].sem, C["s_ve"].v)])
                            self.wait("vector", "s_mm", mv)
                            self.wait("vector", "s_ld", lvx)
                            self.tt(x_t2[:, :], x_t2[:, :], lnw2["o1"], OP.mult)
                            self.tt(x_t2[:, :], x_t2[:, :], lnw2["o2"], OP.subtract)
                            vv = self.tt(x_t2[:, :], x_t2[:, :], ps_wo[:, :], OP.add, inc="s_ve")
                            self.store(g["xs"][oc * P:(oc + 1) * P, c0:c0 + 512], x_t2[:, :],
                                       wait=[(C["s_ve"].sem, vv)])
                            self.wait("tensor", "s_ve", vv)
                            self.wait("vector", "s_st", C["s_st"].v)

    def p2_ffn(self):
        g, nc, C = self.g, self.nc, self.C
        ffnb = self.sb("ffnb", [P, 8])
        lvb = self.load(ffnb[:, :], g["ffn_bias"][:, :])
        with ExitStack() as px:
            sbl = lambda n, s, d=F32: px.enter_context(nc.sbuf_tensor(n, list(s), d))
            wrt = sbl("wrt", [P, 2048], BF16)
            wit = sbl("wit", [P, 2048], BF16)
            nwit = sbl("nwit", [P, 2048], BF16)
            ka = sbl("ka", [P, 512])
            kb = sbl("kb", [P, 512])
            kacc = sbl("kacc", [P, 512])
            x_t = sbl("p2x_t", [P, 2 * 512], BF16)
            xq_t = sbl("p2xq_t", [P, 512], BF16)
            xn2 = sbl("xn2", [P, 8 * 512], BF16)
            ev_t = sbl("ev_t", [P, 512])
            xres = sbl("xres", [P, 512])
            lnw = {}
            for nm in ["m", "v", "t1", "t2", "o1", "o2"]:
                lnw[nm] = sbl("p2lnw_" + nm, [P, 512])
            ps_st = px.enter_context(nc.psum_tensor("p2ps_st", [P, 512], F32))
            ps_e = px.enter_context(nc.psum_tensor("p2ps_e", [P, 512], F32))
            ps_o = px.enter_context(nc.psum_tensor("p2ps_o", [P, 2 * 512], F32))

            for name in ["R", "I"]:
                dst = wrt if name == "R" else wit
                for pc in range(4):
                    for c in range(4):
                        lv = self.load(ka[:, :], g["kron_A" + name][c, pc * P:(pc + 1) * P, :],
                                       wait=[(C["s_ve"].sem, C["s_ve"].v)])
                        lv = self.load(kb[:, :], g["kron_B" + name][c, pc * P:(pc + 1) * P, :])
                        self.wait("vector", "s_ld", lv)
                        if c == 0:
                            vv = self.tt(kacc[:, :], ka[:, :], kb[:, :], OP.mult, inc="s_ve")
                        else:
                            self.tt(ka[:, :], ka[:, :], kb[:, :], OP.mult)
                            vv = self.tt(kacc[:, :], kacc[:, :], ka[:, :], OP.add, inc="s_ve")
                    vv = self.vcopy(dst[:, pc * 512:(pc + 1) * 512], kacc[:, :], inc="s_ve")
            vv = self.tscal(nwit[:, :], wit[:, :], -1.0, None, OP.mult, inc="s_ve")
            self.wait("tensor", "s_ve", vv)

            self.stats_pass(x_t, xq_t, ps_st)
            self.wait("scalar", "s_ld", lvb)

            wln = dict(lnw); wln["ps_m"] = ps_st; wln["ps_e"] = ps_e
            vv = C["s_ve"].v
            for tc in range(TPC // 512):
                c0 = tc * 512
                chv = self.ln_chain(self.stats, c0, wln, "ln")
                self.wait("scalar", "s_ve", C["s_ve"].v)
                for dc in range(8):
                    sl = (dc % 2) * 512
                    lvx = self.load(x_t[:, sl:sl + 512], g["xs"][dc * P:(dc + 1) * P, c0:c0 + 512],
                                    cast=True, wait=[(C["s_ve"].sem, chv if dc < 2 else vv)])
                    self.wait("vector", "s_ld", lvx)
                    self.tt(xn2[:, dc * 512:(dc + 1) * 512], x_t[:, sl:sl + 512], lnw["o1"][:, :], OP.mult)
                    vv = self.tt(xn2[:, dc * 512:(dc + 1) * 512], xn2[:, dc * 512:(dc + 1) * 512],
                                 lnw["o2"][:, :], OP.subtract, inc="s_ve")
                self.wait("tensor", "s_ve", vv)
                for oc in range(8):
                    pso = ps_o[:, (oc % 2) * 512:(oc % 2) * 512 + 512]
                    if oc < 4:
                        parts = [(wrt, 0), (nwit, 4)]
                        ocr = oc
                    else:
                        parts = [(wit, 0), (wrt, 4)]
                        ocr = oc - 4
                    k = 0
                    for wmat, dbase in parts:
                        for finc in range(4):
                            stat = wmat[:, finc * 512 + ocr * P:finc * 512 + (ocr + 1) * P]
                            mv = self.mm(pso, stat, xn2[:, (dbase + finc) * 512:(dbase + finc + 1) * 512],
                                         k == 0, k == 7, inc=("s_mm" if k == 7 else None))
                            k += 1
                    lvr = self.load(xres[:, :], g["xs"][oc * P:(oc + 1) * P, c0:c0 + 512],
                                    wait=[(C["s_st"].sem, C["s_st"].v)])
                    self.wait("vector", "s_mm", mv)
                    self.wait("vector", "s_ld", lvr)
                    self.vcopy(ev_t[:, :], pso)
                    vv2 = self.stt(ev_t[:, :], ev_t[:, :], ffnb[:, oc:oc + 1], xres[:, :],
                                   OP.add, OP.add, inc="s_ve")
                    self.store(g["xs"][oc * P:(oc + 1) * P, c0:c0 + 512], ev_t[:, :],
                               wait=[(C["s_ve"].sem, vv2)])
                    self.wait("tensor", "s_ve", vv2)
                    self.wait("vector", "s_st", C["s_st"].v)

    def p3_attn(self):
        g, nc, C = self.g, self.nc, self.C

        with ExitStack() as px:
            sbl = lambda n, s, d=F32: px.enter_context(nc.sbuf_tensor(n, list(s), d))
            wq3 = sbl("wq3", [P, 8 * DM], BF16)
            wk3 = sbl("wk3", [P, 8 * DM], BF16)
            wv3 = sbl("wv3", [P, 8 * DM], BF16)
            ropc = sbl("ropc", [P, 4 * LSEQ], BF16)
            rops = sbl("rops", [P, 4 * LSEQ], BF16)
            x_t = sbl("p3x_t", [P, 2 * 512], BF16)
            xq_t = sbl("p3xq_t", [P, 512], BF16)
            xn3 = sbl("xn3", [P, 8 * 512], BF16)
            q1 = sbl("q1", [P, 8 * 512], BF16)
            rt1 = sbl("rt1", [P, 512])
            rt2 = sbl("rt2", [P, 512])
            ro_t = sbl("ro_t", [P, 512], BF16)
            lnw = {}
            for nm in ["m", "v", "t1", "t2", "o1", "o2"]:
                lnw[nm] = sbl("p3lnw_" + nm, [P, 512])
            ps_st = px.enter_context(nc.psum_tensor("p3ps_st", [P, 512], F32))
            ps_e = px.enter_context(nc.psum_tensor("p3ps_e", [P, 512], F32))
            ps_o = px.enter_context(nc.psum_tensor("p3ps_o", [P, 2 * 512], F32))

            for dc in range(8):
                self.load(wq3[:, dc * DM:(dc + 1) * DM], g["wq_t"][dc * P:(dc + 1) * P, :], cast=True)
                self.load(wk3[:, dc * DM:(dc + 1) * DM], g["wk_t"][dc * P:(dc + 1) * P, :], cast=True)
                self.load(wv3[:, dc * DM:(dc + 1) * DM], g["wv_t"][dc * P:(dc + 1) * P, :], cast=True)
            for pc in range(4):
                self.load(ropc[:, pc * LSEQ:(pc + 1) * LSEQ], g["c_rope_cos"][pc * P:(pc + 1) * P, :], cast=True)
                lwv = self.load(rops[:, pc * LSEQ:(pc + 1) * LSEQ], g["c_rope_sin"][pc * P:(pc + 1) * P, :], cast=True)
            self.wait("tensor", lwv[0], lwv[1])
            self.wait("vector", lwv[0], lwv[1])

            self.stats_pass(x_t, xq_t, ps_st)

            wln = dict(lnw); wln["ps_m"] = ps_st; wln["ps_e"] = ps_e
            vv = C["s_ve"].v
            for tc in range(TPC // 512):
                c0 = tc * 512
                pos0 = (tc % 4) * 512
                chv = self.ln_chain(self.stats, c0, wln, "ln")
                self.wait("scalar", "s_ve", C["s_ve"].v)
                for dc in range(8):
                    sl = (dc % 2) * 512
                    lvx = self.load(x_t[:, sl:sl + 512], g["xs"][dc * P:(dc + 1) * P, c0:c0 + 512],
                                    cast=True, wait=[(C["s_ve"].sem, chv if dc < 2 else vv)])
                    self.wait("vector", lvx[0], lvx[1])
                    self.tt(xn3[:, dc * 512:(dc + 1) * 512], x_t[:, sl:sl + 512], lnw["o1"][:, :], OP.mult)
                    vv = self.tt(xn3[:, dc * 512:(dc + 1) * 512], xn3[:, dc * 512:(dc + 1) * 512],
                                 lnw["o2"][:, :], OP.subtract, inc="s_ve")
                self.wait("tensor", "s_ve", vv)
                for proj, wmat, dst in [("q", wq3, "sc_q"), ("k", wk3, "sc_k"), ("v", wv3, "sc_v")]:
                    self.wait("scalar", "s_ve", C["s_ve"].v)
                    for oc in range(8):
                        pso = ps_o[:, (oc % 2) * 512:(oc % 2) * 512 + 512]
                        for dc in range(8):
                            stat = wmat[:, dc * DM + oc * P:dc * DM + (oc + 1) * P]
                            mv = self.mm(pso, stat, xn3[:, dc * 512:(dc + 1) * 512], dc == 0, dc == 7,
                                         inc=("s_mm" if dc == 7 else None))
                        self.wait("scalar", "s_mm", mv)
                        av = self.act(q1[:, oc * 512:(oc + 1) * 512], pso, AF.Copy, inc="s_ac")
                        self.wait("tensor", "s_ac", av)
                    self.wait("vector", "s_ac", av)
                    for k in range(4):
                        cc = ropc[:, k * LSEQ + pos0:k * LSEQ + pos0 + 512]
                        ss = rops[:, k * LSEQ + pos0:k * LSEQ + pos0 + 512]
                        qe = q1[:, k * 512:(k + 1) * 512]
                        qo = q1[:, (4 + k) * 512:(5 + k) * 512]
                        self.tt(rt1[:, :], qe, cc, OP.mult)
                        self.tt(rt2[:, :], qo, ss, OP.mult)
                        vv = self.tt(ro_t[:, :], rt1[:, :], rt2[:, :], OP.add, inc="s_ve")
                        self.store(g[dst][k * P:(k + 1) * P, c0:c0 + 512], ro_t[:, :],
                                   wait=[(C["s_ve"].sem, vv)])
                        self.wait("vector", "s_st", C["s_st"].v)
                        self.tt(rt1[:, :], qo, cc, OP.mult)
                        self.tt(rt2[:, :], qe, ss, OP.mult)
                        vv = self.tt(ro_t[:, :], rt1[:, :], rt2[:, :], OP.subtract, inc="s_ve")
                        self.store(g[dst][(4 + k) * P:(5 + k) * P, c0:c0 + 512], ro_t[:, :],
                                   wait=[(C["s_ve"].sem, vv)])
                        self.wait("vector", "s_st", C["s_st"].v)
        self.barrier()

        with ExitStack() as px:
            sbl = lambda n, s, d=F32: px.enter_context(nc.sbuf_tensor(n, list(s), d))
            wqi = sbl("wqi", [P, 8 * DM], BF16)
            wki = sbl("wki", [P, 8 * DM], BF16)
            wvi = sbl("wvi", [P, 8 * DM], BF16)
            bq3 = sbl("bq3", [P, 8])
            bk3 = sbl("bk3", [P, 8])
            bvi_f = sbl("bvi_f", [1, DM])
            bvi_b = sbl("bvi_b", [1, DM], BF16)
            m_t = sbl("p3m_t", [P, 2 * 512], BF16)
            ev_t = sbl("p3ev_t", [P, 512], BF16)
            vt_t = sbl("p3vt_t", [P, DM], BF16)
            ps_a = px.enter_context(nc.psum_tensor("p3ps_a", [P, 2 * 512], F32))
            ps_v = px.enter_context(nc.psum_tensor("p3ps_v", [P, 2 * 512], F32))
            for dc in range(8):
                self.load(wqi[:, dc * DM:(dc + 1) * DM], g["wqi_t"][dc * P:(dc + 1) * P, :], cast=True)
                self.load(wki[:, dc * DM:(dc + 1) * DM], g["wki_t"][dc * P:(dc + 1) * P, :], cast=True)
                lwv = self.load(wvi[:, dc * DM:(dc + 1) * DM], g["wvi_t"][dc * P:(dc + 1) * P, :], cast=True)
            self.load(bq3[:, :], g["bqi"][:, :])
            self.load(bk3[:, :], g["bki"][:, :])
            lvb = self.load(bvi_f[:, :], g["bvi"][:, :])
            self.wait("vector", lvb[0], lvb[1])
            bcv = self.vcopy(bvi_b[:, :], bvi_f[:, :], inc="s_ve")
            self.wait("tensor", "s_ve", bcv)
            self.wait("tensor", lwv[0], lwv[1])
            self.wait("scalar", lvb[0], lvb[1])

            mall = sbl("p3mall", [P, 8 * 512], BF16)
            for src_nm, wmat, bias, dst in [("sc_q", wqi, bq3, "sc_qh"), ("sc_k", wki, bk3, "sc_kh")]:
                for tc in range(TPC // 512):
                    c0 = tc * 512
                    for dc in range(8):
                        lvx = self.load(mall[:, dc * 512:(dc + 1) * 512],
                                        g[src_nm][dc * P:(dc + 1) * P, c0:c0 + 512],
                                        wait=([(C["s_mm"].sem, C["s_mm"].v)] if dc == 0 else ()))
                    self.wait("tensor", lvx[0], lvx[1])
                    for oc in range(8):
                        pso = ps_a[:, (oc % 2) * 512:(oc % 2) * 512 + 512]
                        for dc in range(8):
                            mv = self.mm(pso, wmat[:, dc * DM + oc * P:dc * DM + (oc + 1) * P],
                                         mall[:, dc * 512:(dc + 1) * 512], dc == 0, dc == 7,
                                         inc=("s_mm" if dc == 7 else None))
                        self.wait("vector", "s_mm", mv)
                        av = self.tscal(ev_t[:, :], pso, bias[:, oc:oc + 1],
                                        self.consts["ones_col_f"][:, 0:1], OP.add, OP.mult, inc="s_ve")
                        self.store(g[dst][oc * P:(oc + 1) * P, c0:c0 + 512], ev_t[:, :],
                                   wait=[(C["s_ve"].sem, av)])
                        self.wait("vector", "s_st", C["s_st"].v)
                        self.wait("tensor", "s_ve", av)

            for tt in range(TPC // P):
                c0 = tt * P
                for dc in range(8):
                    lvx = self.load(m_t[:, (dc % 2) * 512:(dc % 2) * 512 + P],
                                    g["sc_v"][dc * P:(dc + 1) * P, c0:c0 + P],
                                    wait=[(C["s_mm"].sem, C["s_mm"].v)])
                    self.wait("tensor", lvx[0], lvx[1])
                    for hh in range(2):
                        self.mm(ps_v[:, hh * 512:(hh + 1) * 512],
                                m_t[:, (dc % 2) * 512:(dc % 2) * 512 + P],
                                wvi[:, dc * DM + hh * 512:dc * DM + (hh + 1) * 512],
                                dc == 0, False,
                                inc=("s_mm" if hh == 1 else None))
                for hh in range(2):
                    mv = self.mm(ps_v[:, hh * 512:(hh + 1) * 512], self.consts["ones_row_b"][:, :],
                                 bvi_b[:, hh * 512:(hh + 1) * 512], False, True,
                                 inc="s_mm")
                self.wait("vector", "s_mm", mv)
                vv = self.vcopy(vt_t[:, :], ps_v[:, :], inc="s_ve")
                self.store(g["sc_vh_tm"][c0:c0 + P, :], vt_t[:, :], wait=[(C["s_ve"].sem, vv)])
                self.wait("vector", "s_st", C["s_st"].v)
                self.wait("tensor", "s_ve", vv)
        self.barrier()

        with ExitStack() as px:
            sbl = lambda n, s, d=F32: px.enter_context(nc.sbuf_tensor(n, list(s), d))
            qh_t = sbl("qh_t", [64, LSEQ], BF16)
            kh_t = sbl("kh_t", [64, LSEQ], BF16)
            vaug = sbl("vaug", [P, 16 * 65], BF16)
            pt_t = sbl("pt_t", [P, 16 * 512], BF16)
            osb = sbl("osb", [64, 512], BF16)
            ofin = sbl("ofin", [64, 512], BF16)
            lrec = sbl("lrec", [P, 512])
            ps_s = px.enter_context(nc.psum_tensor("p3ps_s", [P, 2 * 512], F32))
            ps_av = px.enter_context(nc.psum_tensor("p3ps_av", [P, 512], F32))
            ps_l = px.enter_context(nc.psum_tensor("p3ps_l", [1, 512], F32))
            ps_lr = px.enter_context(nc.psum_tensor("p3ps_lr", [P, 512], F32))
            for kc in range(16):
                self.vmemset(vaug[:, kc * 65 + 64:kc * 65 + 65], 1.0)
            ivv = self.emit("vector", lambda e: e.engine_nop(), inc="s_ve")
            self.wait("sync", "s_ve", ivv)

            for seq in range(NSEQ):
                for h in range(NHEAD):
                    sb0 = seq * LSEQ
                    lq = self.load(qh_t[:, :], g["sc_qh"][h * HD:(h + 1) * HD, sb0:sb0 + LSEQ],
                                   wait=[(C["s_mm"].sem, C["s_mm"].v)])
                    lk = self.load(kh_t[:, :], g["sc_kh"][h * HD:(h + 1) * HD, sb0:sb0 + LSEQ])
                    vsrc = g["sc_vh_tm"][sb0:sb0 + LSEQ, h * HD:(h + 1) * HD]
                    vdst = vaug[:, 0:16 * 65].rearrange("p (k c) -> p k c", c=65)[:, :, 0:HD]
                    lv3 = self.load(vdst, vsrc.rearrange("(k p) c -> p k c", p=P))
                    self.wait("tensor", lq[0], lq[1])
                    self.wait("tensor", lv3[0], lv3[1])
                    for qt in range(4):
                        q0 = qt * 512
                        nkc = 4 * qt + 4
                        avl = C["s_ac"].v
                        vvl = C["s_ve"].v
                        for kc in range(nkc):
                            pss = ps_s[:, (kc % 2) * 512:(kc % 2) * 512 + 512]
                            mv = self.mm(pss, kh_t[:, kc * P:(kc + 1) * P], qh_t[:, q0:q0 + 512],
                                         True, True, inc="s_mm")
                            self.wait("scalar", "s_mm", mv)
                            avl = self.act(pt_t[:, kc * 512:(kc + 1) * 512], pss, AF.Exp,
                                           scale=0.125, inc="s_ac")
                            self.wait("tensor", "s_ac", avl)
                            if kc >= 4 * qt:
                                self.wait("vector", "s_ac", avl)
                                vvl = self.tt(pt_t[:, kc * 512:(kc + 1) * 512],
                                              pt_t[:, kc * 512:(kc + 1) * 512],
                                              self.consts["diagmask"][:, (kc - 4 * qt) * 512:(kc - 4 * qt + 1) * 512],
                                              OP.mult, inc="s_ve")
                        self.wait("tensor", "s_ve", vvl)
                        self.wait("tensor", "s_ac", avl)
                        for kc in range(nkc):
                            self.mm(ps_av[0:64, :], vaug[:, kc * 65:kc * 65 + 64],
                                    pt_t[:, kc * 512:(kc + 1) * 512], kc == 0, kc == nkc - 1)
                            mv = self.mm(ps_l[0:1, :], self.consts["stat_sx"][:, 0:1],
                                         pt_t[:, kc * 512:(kc + 1) * 512], kc == 0, kc == nkc - 1,
                                         inc=("s_mm" if kc == nkc - 1 else None))
                        self.wait("vector", "s_mm", mv)
                        self.recip(lrec[0:1, :], ps_l[0:1, :])
                        vv = self.vcopy(osb[:, :], ps_av[0:64, :], inc="s_ve")
                        self.wait("tensor", "s_ve", vv)
                        mv = self.mm(ps_lr[:, :], self.consts["ones_row_f"][:, :], lrec[0:1, :],
                                     True, True, inc="s_mm")
                        self.wait("vector", "s_mm", mv)
                        vv = self.tt(ofin[:, :], osb[:, :], ps_lr[0:64, :], OP.mult, inc="s_ve")
                        self.store(g["sc_o"][h * HD:(h + 1) * HD, sb0 + q0:sb0 + q0 + 512], ofin[:, :],
                                   wait=[(C["s_ve"].sem, vv)])
                        self.wait("vector", "s_st", C["s_st"].v)
                        self.wait("tensor", "s_ve", vv)
        self.barrier()

        with ExitStack() as px:
            sbl = lambda n, s, d=F32: px.enter_context(nc.sbuf_tensor(n, list(s), d))
            wo3 = sbl("wo3", [P, 8 * DM], BF16)
            ob3 = sbl("ob3", [P, 8])
            mall = sbl("p3dmall", [P, 8 * 512], BF16)
            xres = sbl("p3xres", [P, 512])
            for dc in range(8):
                lwv = self.load(wo3[:, dc * DM:(dc + 1) * DM], g["out_w_t"][dc * P:(dc + 1) * P, :], cast=True)
            lvb = self.load(ob3[:, :], g["out_b"][:, :])
            self.wait("tensor", lwv[0], lwv[1])
            self.wait("vector", lvb[0], lvb[1])
            ps_a = px.enter_context(nc.psum_tensor("p3dps_a", [P, 2 * 512], F32))
            for tc in range(TPC // 512):
                c0 = tc * 512
                for dc in range(8):
                    lvx = self.load(mall[:, dc * 512:(dc + 1) * 512],
                                    g["sc_o"][dc * P:(dc + 1) * P, c0:c0 + 512],
                                    wait=([(C["s_mm"].sem, C["s_mm"].v)] if dc == 0 else ()))
                self.wait("tensor", lvx[0], lvx[1])
                for oc in range(8):
                    pso = ps_a[:, (oc % 2) * 512:(oc % 2) * 512 + 512]
                    for dc in range(8):
                        mv = self.mm(pso, wo3[:, dc * DM + oc * P:dc * DM + (oc + 1) * P],
                                     mall[:, dc * 512:(dc + 1) * 512], dc == 0, dc == 7,
                                     inc=("s_mm" if dc == 7 else None))
                    lvr = self.load(xres[:, :], g["xs"][oc * P:(oc + 1) * P, c0:c0 + 512],
                                    wait=[(C["s_st"].sem, C["s_st"].v)])
                    self.wait("vector", "s_mm", mv)
                    self.wait("vector", lvr[0], lvr[1])
                    vv2 = self.stt(xres[:, :], pso, ob3[:, oc:oc + 1], xres[:, :],
                                   OP.add, OP.add, inc="s_ve")
                    self.store(g["xs"][oc * P:(oc + 1) * P, c0:c0 + 512], xres[:, :],
                               wait=[(C["s_ve"].sem, vv2)])
                    self.wait("tensor", "s_ve", vv2)


    def p4_moe(self):
        g, nc, C = self.g, self.nc, self.C
        NT = TPC // P

        oh_all = self.sb("oh_all", [P, NT * 16])
        excl_sb = self.sb("excl_sb", [P, NT * 8])
        route_sb = self.sb("route_sb", [P, NT * 8])
        idx_all = self.sb("idx_all", [P, NT * 2], I32)
        tot_sb = self.sb("tot_sb", [1, NT * 8])
        off_sb = self.sb("off_sb", [1, (NT + 1) * 8])
        base_sb = self.sb("base_sb", [1, 8])
        brep_sb = self.sb("brep_sb", [P, 8])
        agsb = self.sb("agsb", [8, 8])

        with ExitStack() as px:
            sbl = lambda n, s, d=F32: px.enter_context(nc.sbuf_tensor(n, list(s), d))
            wg_sb = sbl("wg_sb", [P, 8 * 8], BF16)
            x_t32 = sbl("p4x32", [P, 2 * 512])
            xq_t = sbl("p4xq", [P, 512], BF16)
            xn4 = sbl("p4xn4", [P, 8 * 512], BF16)
            x3tm = sbl("x3tm", [P, 4 * DM])
            xn4tm = sbl("xn4tm", [P, 4 * DM], BF16)
            oh12b = sbl("oh12b", [P, 8], BF16)
            w8a = sbl("p4w8a", [P, 8])
            w8b = sbl("p4w8b", [P, 8])
            w1a = sbl("p4w1a", [P, 1])
            w1b = sbl("p4w1b", [P, 1])
            w1c = sbl("p4w1c", [P, 1])
            lnw = {}
            for nm in ["m", "v", "t1", "t2", "o1", "o2"]:
                lnw[nm] = sbl("p4lnw_" + nm, [P, 512])
            ps_st = px.enter_context(nc.psum_tensor("p4ps_st", [P, 512], F32))
            ps_e = px.enter_context(nc.psum_tensor("p4ps_e", [P, 512], F32))
            ps_tp = px.enter_context(nc.psum_tensor("p4ps_tp", [P, 512], F32))
            ps_tpb = px.enter_context(nc.psum_tensor("p4ps_tpb", [P, P], BF16))
            ps_sm = px.enter_context(nc.psum_tensor("p4ps_sm", [P, 512], F32))

            for dc in range(8):
                lwv = self.load(wg_sb[:, dc * 8:(dc + 1) * 8], g["Wg"][dc * P:(dc + 1) * P, :], cast=True)
            self.wait("tensor", lwv[0], lwv[1])
            self.stats_pass(x_t32[:, :].bitcast(BF16)[:, 0:1024], xq_t, ps_st)

            wln = dict(lnw); wln["ps_m"] = ps_st; wln["ps_e"] = ps_e
            vv = C["s_ve"].v
            for tc in range(TPC // 512):
                c0 = tc * 512
                chv = self.ln_chain(self.stats, c0, wln, "ln")
                self.wait("scalar", "s_ve", C["s_ve"].v)
                for dc in range(8):
                    sl = (dc % 2) * 512
                    lvx = self.load(x_t32[:, sl:sl + 512], g["xs"][dc * P:(dc + 1) * P, c0:c0 + 512],
                                    wait=[(C["s_ve"].sem, chv if dc < 2 else vv),
                                          (C["s_mm"].sem, C["s_mm"].v)])
                    self.wait("vector", lvx[0], lvx[1])
                    self.tt(xn4[:, dc * 512:(dc + 1) * 512], x_t32[:, sl:sl + 512], lnw["o1"][:, :], OP.mult)
                    vv = self.tt(xn4[:, dc * 512:(dc + 1) * 512], xn4[:, dc * 512:(dc + 1) * 512],
                                 lnw["o2"][:, :], OP.subtract, inc="s_ve")
                    self.wait("tensor", "s_ve", vv)
                    for sub in range(4):
                        mvt = self.transpose_mm(ps_tp[:, 0:P], x_t32[:, sl + sub * P:sl + (sub + 1) * P],
                                                False, inc="s_mm")
                        self.wait("vector", "s_mm", mvt)
                        vv = self.vcopy(x3tm[:, sub * DM + dc * P:sub * DM + (dc + 1) * P],
                                        ps_tp[:, 0:P], inc="s_ve")
                        self.wait("tensor", "s_ve", vv)
                        mvt = self.transpose_mm(ps_tpb[:, 0:P],
                                                xn4[:, dc * 512 + sub * P:dc * 512 + (sub + 1) * P],
                                                True, inc="s_mm")
                        self.wait("vector", "s_mm", mvt)
                        vv = self.vcopy(xn4tm[:, sub * DM + dc * P:sub * DM + (dc + 1) * P],
                                        ps_tpb[:, 0:P], inc="s_ve")
                        self.wait("tensor", "s_ve", vv)
                for sub in range(4):
                    tile = tc * 4 + sub
                    self.store(g["sc_x3_tm"][tile * P:(tile + 1) * P, :], x3tm[:, sub * DM:(sub + 1) * DM],
                               wait=[(C["s_ve"].sem, vv)])
                    self.store(g["sc_xn4_tm"][tile * P:(tile + 1) * P, :], xn4tm[:, sub * DM:(sub + 1) * DM])
                self.wait("vector", "s_st", C["s_st"].v)
                for sub in range(4):
                    tile = tc * 4 + sub
                    for dc in range(8):
                        mv = self.mm(ps_sm[:, 0:8], xn4[:, dc * 512 + sub * P:dc * 512 + (sub + 1) * P],
                                     wg_sb[:, dc * 8:(dc + 1) * 8], dc == 0, dc == 7,
                                     inc=("s_mm" if dc == 7 else None))
                    self.wait("vector", "s_mm", mv)
                    lgt = w8a
                    self.vcopy(lgt[:, :], ps_sm[:, 0:8])
                    self.tscal(w8b[:, :], self.consts["iota8"][:, :], 1e-6, None, OP.mult)
                    self.tt(w8b[:, :], lgt[:, :], w8b[:, :], OP.subtract)
                    self.vreduce(w1a[:, 0:1], w8b[:, :], OP.max)
                    self.tscal(oh_all[:, tile * 16:tile * 16 + 8], w8b[:, :], w1a[:, 0:1],
                               self.consts["ones_col_f"][:, 0:1], OP.is_equal, OP.mult)
                    self.tscal(lnw["t1"][:, 0:8], oh_all[:, tile * 16:tile * 16 + 8], 1e9, None, OP.mult)
                    self.tt(lnw["t1"][:, 0:8], w8b[:, :], lnw["t1"][:, 0:8], OP.subtract)
                    self.vreduce(w1a[:, 0:1], lnw["t1"][:, 0:8], OP.max)
                    self.tscal(oh_all[:, tile * 16 + 8:tile * 16 + 16], lnw["t1"][:, 0:8], w1a[:, 0:1],
                               self.consts["ones_col_f"][:, 0:1], OP.is_equal, OP.mult)
                    self.tt(w8b[:, :], oh_all[:, tile * 16:tile * 16 + 8], lgt[:, :], OP.mult)
                    self.vreduce(w1a[:, 0:1], w8b[:, :])
                    self.tt(w8b[:, :], oh_all[:, tile * 16 + 8:tile * 16 + 16], lgt[:, :], OP.mult)
                    self.vreduce(w1b[:, 0:1], w8b[:, :])
                    vv = self.tt(w1b[:, 0:1], w1b[:, 0:1], w1a[:, 0:1], OP.subtract, inc="s_ve")
                    self.wait("scalar", "s_ve", vv)
                    av = self.act(w1c[:, 0:1], w1b[:, 0:1], AF.Exp, inc="s_ac")
                    self.wait("vector", "s_ac", av)
                    self.tscal(w1a[:, 0:1], w1c[:, 0:1], 1.0, None, OP.add)
                    self.recip(w1a[:, 0:1], w1a[:, 0:1])
                    self.vcopy(route_sb[:, tile * 8:tile * 8 + 1], w1a[:, 0:1])
                    self.tt(route_sb[:, tile * 8 + 1:tile * 8 + 2], w1c[:, 0:1], w1a[:, 0:1], OP.mult)
                    self.tt(w8a[:, :], oh_all[:, tile * 16:tile * 16 + 8],
                            oh_all[:, tile * 16 + 8:tile * 16 + 16], OP.add)
                    vv = self.vcopy(oh12b[:, :], w8a[:, :], inc="s_ve")
                    self.wait("tensor", "s_ve", vv)
                    self.mm(ps_sm[:, 16:24], self.consts["lt_b"][:, :], oh12b[:, :], True, True)
                    mv = self.mm(ps_sm[0:1, 32:40], self.consts["stat_sx"][:, 0:1], oh12b[:, :],
                                 True, True, inc="s_mm")
                    self.wait("vector", "s_mm", mv)
                    self.vcopy(excl_sb[:, tile * 8:(tile + 1) * 8], ps_sm[:, 16:24])
                    vv = self.vcopy(tot_sb[0:1, tile * 8:(tile + 1) * 8], ps_sm[0:1, 32:40], inc="s_ve")
                    self.wait("tensor", "s_ve", vv)

            self.vmemset(off_sb[0:1, 0:8], 0.0)
            for k in range(NT):
                vv = self.tt(off_sb[0:1, (k + 1) * 8:(k + 2) * 8], off_sb[0:1, k * 8:(k + 1) * 8],
                             tot_sb[0:1, k * 8:(k + 1) * 8], OP.add, inc="s_ve")
            stv = self.store(g["agin"][:, :], off_sb[0:1, NT * 8:(NT + 1) * 8],
                             wait=[(C["s_ve"].sem, vv)])
            def _coll(e, i=g["agin"][:, :], o=g["agout"][:, :], sem=C["s_gp"].sem, w=(C["s_st"].sem, stv)):
                e.wait_ge(w[0], w[1])
                e.collective_compute("AllGather", OP.bypass, replica_groups=[list(range(NCORES))],
                                     ins=[i], outs=[o]).then_inc(sem, 1)
            gv = self.C["s_gp"].inc(1)
            self.plan.em("gpsimd", _coll)
            lva = self.load(agsb[:, :], g["agout"][:, :], wait=[(C["s_gp"].sem, gv)])
            self.wait("tensor", lva[0], lva[1])
            mv = self.mm(ps_sm[0:1, 48:56], self.consts["coremask"][:, 0:1], agsb[:, :],
                         True, True, inc="s_mm")
            self.wait("vector", "s_mm", mv)
            vv = self.vcopy(base_sb[0:1, :], ps_sm[0:1, 48:56], inc="s_ve")
            self.wait("tensor", "s_ve", vv)
            mv = self.mm(ps_sm[:, 64:72], self.consts["ones_row_f"][:, :], base_sb[0:1, :],
                         True, True, inc="s_mm")
            self.wait("vector", "s_mm", mv)
            vv = self.vcopy(brep_sb[:, :], ps_sm[:, 64:72], inc="s_ve")
            self.wait("tensor", "s_ve", vv)

            for tile in range(NT):
                mv = self.mm(ps_sm[:, 80:88], self.consts["ones_row_f"][:, :],
                             off_sb[0:1, tile * 8:(tile + 1) * 8], True, True, inc="s_mm")
                self.wait("vector", "s_mm", mv)
                for k in range(2):
                    ohk = oh_all[:, tile * 16 + 8 * k:tile * 16 + 8 * (k + 1)]
                    self.tt(w8a[:, :], ohk, excl_sb[:, tile * 8:(tile + 1) * 8], OP.mult)
                    self.vreduce(w1a[:, 0:1], w8a[:, :])
                    self.tt(w8a[:, :], ohk, ps_sm[:, 80:88], OP.mult)
                    self.vreduce(w1b[:, 0:1], w8a[:, :])
                    self.tt(w1a[:, 0:1], w1a[:, 0:1], w1b[:, 0:1], OP.add)
                    self.tt(w8a[:, :], ohk, brep_sb[:, :], OP.mult)
                    self.vreduce(w1b[:, 0:1], w8a[:, :])
                    self.tt(w1b[:, 0:1], w1a[:, 0:1], w1b[:, 0:1], OP.add)
                    self.tscal(w1b[:, 0:1], w1b[:, 0:1], float(CAPG), self.consts["ones_col_f"][:, 0:1],
                               OP.is_lt, OP.mult)
                    self.tscal(w1c[:, 0:1], w1a[:, 0:1], float(CAPL), self.consts["ones_col_f"][:, 0:1],
                               OP.is_lt, OP.mult)
                    self.tt(w1b[:, 0:1], w1b[:, 0:1], w1c[:, 0:1], OP.mult)
                    gcol = route_sb[:, tile * 8 + k:tile * 8 + k + 1]
                    self.tt(gcol, gcol, w1b[:, 0:1], OP.mult)
                    self.tt(w8a[:, :], ohk, self.consts["iota8"][:, :], OP.mult)
                    self.vreduce(w1b[:, 0:1], w8a[:, :])
                    self.tscal(w1b[:, 0:1], w1b[:, 0:1], float(CAPL), None, OP.mult)
                    self.tt(w1b[:, 0:1], w1b[:, 0:1], w1a[:, 0:1], OP.add)
                    self.tscal(w1b[:, 0:1], w1b[:, 0:1], -float(TRASH), None, OP.add)
                    self.tt(w1b[:, 0:1], w1b[:, 0:1], w1c[:, 0:1], OP.mult)
                    self.tscal(w1b[:, 0:1], w1b[:, 0:1], float(TRASH), None, OP.add)
                    vv = self.vcopy(idx_all[:, tile * 2 + k:tile * 2 + k + 1], w1b[:, 0:1], inc="s_ve")
                self.wait("tensor", "s_ve", vv)

            scv = C["s_ve"].v
            sgl = C["s_lg"].v
            for tile in range(NT):
                sl = (tile % 4) * DM
                ltl = self.load(xn4tm[:, sl:sl + DM], g["sc_xn4_tm"][tile * P:(tile + 1) * P, :],
                                wait=[(C["s_lg"].sem, sgl)])
                for k in range(2):
                    tgt = self.C["s_lg"].inc(16)
                    def _sc(e, t=tile, kk=k, s0=sl, w=(C["s_ve"].sem, scv),
                            wl=(C["s_ld"].sem, ltl[1]), sem=C["s_lg"].sem, first=(k == 0),
                            p=tgt - 16):
                        e.wait_ge(sem, p)
                        if first:
                            e.wait_ge(w[0], w[1])
                            e.wait_ge(wl[0], wl[1])
                        e.indirect_dma_start(
                            out=g["sc_ein"][:, :],
                            out_offset=IOOA(idx_all[:, t * 2 + kk:t * 2 + kk + 1], 0),
                            in_=xn4tm[:, s0:s0 + DM], in_offset=None,
                        ).then_inc(sem, 16)
                    self.plan.em("gpsimd", _sc)
                if tile % 4 == 3:
                    sgl = self.C["s_lg"].v
            scdone = ("s_lg", self.C["s_lg"].v)
        self.barrier()

        with ExitStack() as px:
            sbl = lambda n, s, d=F32: px.enter_context(nc.sbuf_tensor(n, list(s), d))
            w1sb = sbl("w1sb", [P, 8 * 4096], BF16)
            xe = sbl("xe", [P, 8 * CAPL], BF16)
            ghb = sbl("ghb", [P, 2 * CAPL], BF16)
            ght = sbl("ght", [P, 2 * P], BF16)
            b1sb = sbl("b1sb", [P, 32])
            evb = sbl("evb", [P, DM], BF16)
            b2row = sbl("b2row", [1, DM], BF16)
            b2f = sbl("b2f", [1, DM])
            ps_h = px.enter_context(nc.psum_tensor("p4ps_h", [P, 2 * 512], F32))
            ps_y = px.enter_context(nc.psum_tensor("p4ps_y", [P, 2 * 512], F32))

            for e in range(NEXP):
                for dc in range(8):
                    lw = self.load(w1sb[:, dc * 4096:(dc + 1) * 4096],
                                   g["W1"][e, dc * P:(dc + 1) * P, :], cast=True,
                                   wait=([(C["s_lg"].sem, scdone[1]), (C["s_mm"].sem, C["s_mm"].v)] if dc == 0 else ()))
                lb = self.load(b1sb[:, :], g["b1"][e, :, :])
                lb2 = self.load(b2f[:, :], g["b2"][e:e + 1, :])
                for dc in range(8):
                    lxe = ("s_ld", self.C["s_ld"].inc(16))
                    def _tp(e2, o=xe[:, dc * CAPL:(dc + 1) * CAPL],
                            i=g["sc_ein"][e * CAPL:(e + 1) * CAPL, dc * P:(dc + 1) * P],
                            sem=C["s_ld"].sem, first=(dc == 0),
                            w1=(C["s_lg"].sem, scdone[1]), w2=(C["s_mm"].sem, C["s_mm"].v),
                            p=lxe[1] - 16):
                        e2.wait_ge(sem, p)
                        if first:
                            e2.wait_ge(w1[0], w1[1])
                            e2.wait_ge(w2[0], w2[1])
                        e2.dma_start(out=o, in_=i, transpose=True).then_inc(sem, 16)
                    self.plan.em("sync", _tp)
                self.wait("vector", lb2[0], lb2[1])
                bcv = self.vcopy(b2row[:, :], b2f[:, :], inc="s_ve")
                self.wait("tensor", "s_ve", bcv)
                self.wait("tensor", lw[0], lw[1])
                self.wait("tensor", lxe[0], lxe[1])
                self.wait("scalar", lb[0], lb[1])
                for hc in range(32):
                    for sl in range(4):
                        psh = ps_h[:, (sl % 2) * 512:(sl % 2) * 512 + 448]
                        for dc in range(8):
                            mv = self.mm(psh, w1sb[:, dc * 4096 + hc * P:dc * 4096 + (hc + 1) * P],
                                         xe[:, dc * CAPL + sl * 448:dc * CAPL + (sl + 1) * 448],
                                         dc == 0, dc == 7, inc=("s_mm" if dc == 7 else None))
                        self.wait("scalar", "s_mm", mv)
                        av = self.act(ghb[:, sl * 448:(sl + 1) * 448], psh, AF.Gelu_apprx_tanh,
                                      bias=b1sb[:, hc:hc + 1], inc="s_ac")
                        self.wait("tensor", "s_ac", av)
                    avv = C["s_ac"].v
                    self.store(g["sc_gh"][hc * P:(hc + 1) * P, :], ghb[:, 0:CAPL],
                               wait=[(C["s_ac"].sem, avv)])
                    self.wait("scalar", "s_st", C["s_st"].v)
                for hc2 in range(8):
                    lw = self.load(w1sb[:, hc2 * 4096:(hc2 + 1) * 4096].rearrange("p (a c) -> p a c", a=4),
                                   g["W2"][e, hc2 * 512:(hc2 + 1) * 512, :].rearrange("(a b) c -> b a c", b=P),
                                   cast=True,
                                   wait=([(C["s_mm"].sem, C["s_mm"].v)] if hc2 == 0 else ()))
                self.wait("tensor", lw[0], lw[1])
                slot_mv = [C["s_mm"].v, C["s_mm"].v]
                for sg in range(14):
                    for hc in range(32):
                        sl = hc % 2
                        lgh = self.load(ght[:, sl * P:sl * P + P],
                                        g["sc_gh"][hc * P:(hc + 1) * P, sg * P:(sg + 1) * P],
                                        wait=[(C["s_mm"].sem, slot_mv[sl])])
                        self.wait("tensor", lgh[0], lgh[1])
                        for half in range(2):
                            w2slice = w1sb[:, (hc // 4) * 4096 + (hc % 4) * 1024 + half * 512:
                                           (hc // 4) * 4096 + (hc % 4) * 1024 + (half + 1) * 512]
                            mv = self.mm(ps_y[:, half * 512:(half + 1) * 512],
                                         ght[:, sl * P:sl * P + P], w2slice,
                                         hc == 0, False,
                                         inc=("s_mm" if half == 1 else None))
                        slot_mv[sl] = mv
                    for half in range(2):
                        mv = self.mm(ps_y[:, half * 512:(half + 1) * 512], self.consts["ones_row_b"][:, :],
                                     b2row[0:1, half * 512:(half + 1) * 512], False, True, inc="s_mm")
                    self.wait("vector", "s_mm", mv)
                    vv = self.vcopy(evb[:, :], ps_y[:, 0:1024], inc="s_ve")
                    self.wait("tensor", "s_ve", vv)
                    self.store(g["sc_eout"][e * CAPL + sg * P:e * CAPL + (sg + 1) * P, :], evb[:, :],
                               wait=[(C["s_ve"].sem, vv)])
                    self.wait("vector", "s_st", C["s_st"].v)
        self.barrier()

        with ExitStack() as px:
            sbl = lambda n, s, d=F32: px.enter_context(nc.sbuf_tensor(n, list(s), d))
            y0 = sbl("p4y0", [P, DM], BF16)
            y1 = sbl("p4y1", [P, DM], BF16)
            x3t = sbl("p4x3t", [P, DM])
            acc = sbl("p4acc", [P, DM])
            zrow = sbl("p4zrow", [1, DM], BF16)
            self.vmemset(zrow[:, :], 0.0)
            zv = self.emit("vector", lambda e: e.engine_nop(), inc="s_ve")
            self.store(g["sc_eout"][TRASH:TRASH + 1, :], zrow[:, :], wait=[(C["s_ve"].sem, zv)])
            trv = ("s_st", C["s_st"].v)
            for tile in range(NT):
                vvp = C["s_ve"].v
                lg0 = ("s_lg", self.C["s_lg"].inc(16))
                def _g0(e, t=tile, w=(C["s_ve"].sem, vvp), w2=(C["s_st"].sem, trv[1]),
                        sem=C["s_lg"].sem, p=lg0[1] - 16):
                    e.wait_ge(sem, p)
                    e.wait_ge(w[0], w[1])
                    e.wait_ge(w2[0], w2[1])
                    e.indirect_dma_start(out=y0[:, :], out_offset=None, in_=g["sc_eout"][:, :],
                                         in_offset=IOOA(idx_all[:, t * 2:t * 2 + 1], 0)).then_inc(sem, 16)
                self.plan.em("gpsimd", _g0)
                lg1 = ("s_lg", self.C["s_lg"].inc(16))
                def _g1(e, t=tile, sem=C["s_lg"].sem, p=lg1[1] - 16):
                    e.wait_ge(sem, p)
                    e.indirect_dma_start(out=y1[:, :], out_offset=None, in_=g["sc_eout"][:, :],
                                         in_offset=IOOA(idx_all[:, t * 2 + 1:t * 2 + 2], 0)).then_inc(sem, 16)
                self.plan.em("gpsimd", _g1)
                lx3 = self.load(x3t[:, :], g["sc_x3_tm"][tile * P:(tile + 1) * P, :],
                                wait=[(C["s_ve"].sem, vvp)])
                self.wait("vector", lg1[0], lg1[1])
                self.wait("vector", lx3[0], lx3[1])
                self.stt(acc[:, :], y0[:, :], route_sb[:, tile * 8:tile * 8 + 1], x3t[:, :],
                         OP.mult, OP.add)
                vv = self.stt(acc[:, :], y1[:, :], route_sb[:, tile * 8 + 1:tile * 8 + 2], acc[:, :],
                              OP.mult, OP.add, inc="s_ve")
                seq = (tile * P) // LSEQ
                r0 = (tile * P) % LSEQ
                self.store(g["out"][seq, r0:r0 + P, :], acc[:, :], wait=[(C["s_ve"].sem, vv)])


def kernel(**inputs):
    in_maps = _host_prepare(inputs)
    nc = build_nc()
    res = run_bass_kernel_spmd(nc, in_maps, core_ids=list(range(NCORES)))
    out = np.concatenate([np.asarray(r["out"]) for r in res.results], axis=0)
    return out.astype(np.float32)
